# version 05
# baseline (speedup 1.0000x reference)
import numpy as np
from contextlib import ExitStack
import concourse.bass as bass
import concourse.mybir as mybir
from concourse.bass_utils import run_bass_kernel_spmd

F32 = mybir.dt.float32
F32R = mybir.dt.float32r
AF = mybir.ActivationFunctionType
OP = mybir.AluOpType

NCORE = 8
BC = 2
T = 2048
KD = 8
KI = 16
NST = 16
TCW = 512
NTC = T // TCW
G = 2
NGRP = KI // G
EPS = 1e-5


class Sem:
    def __init__(self, st, nc, name):
        self.h = st.enter_context(nc.semaphore(name))
        self.n = 0

    def inc(self, bi, amt=1):
        bi.then_inc(self.h, amt)
        self.n += amt
        return self.n


class SemGroup:
    def __init__(self, st, nc, name, k):
        self.sems = [Sem(st, nc, f"{name}{i}") for i in range(k)]
        self.i = 0

    def nxt(self):
        s = self.sems[self.i]
        self.i = (self.i + 1) % len(self.sems)
        return s


class Buf:
    CUR = ['?']
    WAITED = {}

    @classmethod
    def _wait(cls, eng, s, t):
        key = (cls.CUR[0], id(s))
        if cls.WAITED.get(key, -1) >= t:
            return
        eng.wait_ge(s.h, t)
        cls.WAITED[key] = t

    def __init__(self, ap):
        self.ap = ap
        self.ready = None
        self.cons = {}

    def acquire(self, eng):
        for s, t in self.cons.items():
            Buf._wait(eng, s, t)
        self.cons = {}

    def produced(self, s, tok):
        self.ready = (s, tok)

    def wait_ready(self, eng):
        if self.ready is not None:
            s, t = self.ready
            Buf._wait(eng, s, t)

    def consumed(self, s, tok):
        if s in self.cons:
            self.cons[s] = max(self.cons[s], tok)
        else:
            self.cons[s] = tok


def build_nc():
    nc = bass.Bass("TRN2", target_bir_lowering=False)

    def din(name, shape):
        return nc.dram_tensor(name, shape, F32, kind="ExternalInput").ap()

    def dout(name, shape):
        return nc.dram_tensor(name, shape, F32, kind="ExternalOutput").ap()

    def dint(name, shape):
        return nc.dram_tensor(name, shape, F32, kind="Internal").ap()

    xfm = din("xfm", [BC, KD, 128, T])
    W = {}
    for l in (0, 1):
        W[f"inwT{l}"] = din(f"inwT{l}", [32, 128, 8 * 128])
        W[f"outwT{l}"] = din(f"outwT{l}", [8, 128, 16 * 128])
        W[f"dtwT{l}"] = din(f"dtwT{l}", [64, 2048])
        W[f"xpwT{l}"] = din(f"xpwT{l}", [128, 16 * 96])
        W[f"convw{l}"] = din(f"convw{l}", [128, 64])
        W[f"convb{l}"] = din(f"convb{l}", [128, 16])
        W[f"dtb{l}"] = din(f"dtb{l}", [128, 16])
        W[f"dsk{l}"] = din(f"dsk{l}", [128, 16])
        W[f"alog{l}"] = din(f"alog{l}", [128, 256])
        W[f"lnw{l}"] = din(f"lnw{l}", [128, 8])
        W[f"lnb{l}"] = din(f"lnb{l}", [128, 8])
    ffnw = din("ffnw", [1024, 1024])
    ffna1 = din("ffna1", [1024, 1024])
    ffna2 = din("ffna2", [1024, 1024])
    ffnb = din("ffnb", [128, 8])
    onesr = din("onesr", [1, 128])
    onesc = din("onesc", [128, 1])
    cst = din("cst", [128, 4])

    outfm = dout("outfm", [BC, KD, 128, T])

    xn_d = dint("xn_d", [BC, KD, 128, T])
    ffn_d = dint("ffn_d", [BC, KD, 128, T])
    h1_d = dint("h1_d", [BC, KD, 128, T])
    h2_d = dint("h2_d", [BC, KD, 128, T])
    m1_d = dint("m1_d", [BC, KD, 128, T])
    m2_d = dint("m2_d", [BC, KD, 128, T])
    u_d = dint("u_d", [BC, KI, 128, T])
    zs_d = dint("zs_d", [BC, KI, 128, T])
    yp_d = dint("yp_d", [BC, KI, 128, T])

    st = ExitStack()
    st.__enter__()

    def mksb(name, shape, dt=F32):
        return Buf(st.enter_context(nc.sbuf_tensor(name, shape, dt)).ap())

    NPOOL = 13
    pool = [mksb(f"pl{i}", [128, T]) for i in range(NPOOL)]
    conv_sb = pool[11]
    xdbl = [mksb(f"xdbl{b}", [96, T], F32R) for b in range(BC)]
    wt_in = mksb("wt_in", [128, 1024])
    wt_f = [mksb("wt_f0", [128, 1024], F32R), mksb("wt_f1", [128, 1024], F32R)]
    ld_r = [mksb("ld_r0", [128, T], F32R), mksb("ld_r1", [128, T], F32R)]
    wt_out = [mksb("wt_out0", [128, 2048], F32R), mksb("wt_out1", [128, 2048], F32R)]
    xpwT_sb = mksb("xpwT_sb", [128, 1536], F32R)
    dtw1 = mksb("dtw1", [64, 128], F32R)
    u_r = mksb("u_r", [128, T], F32R)
    onesr_r = mksb("onesr_r", [1, 128], F32R)
    A_sb = mksb("A_sb", [128, 256])
    convw_sb = mksb("convw_sb", [128, 64])
    convb_sb = mksb("convb_sb", [128, 16])
    dtb_sb = mksb("dtb_sb", [128, 16])
    dsk_sb = mksb("dsk_sb", [128, 16])
    lnw_sb = mksb("lnw_sb", [128, 8])
    lnb_sb = mksb("lnb_sb", [128, 8])
    ffnb_sb = mksb("ffnb_sb", [128, 8])
    onesc_sb = mksb("onesc_sb", [128, 1])
    cst_sb = mksb("cst_sb", [128, 4])
    stgB = mksb("stgB", [1, T], F32R)
    stgC = mksb("stgC", [1, T], F32R)
    smu = mksb("smu", [1, T], F32R)
    srr = stgC

    ps_a = Buf(st.enter_context(nc.psum_tensor("ps_a", [128, T], F32)).ap())
    ps_b = [Buf(st.enter_context(nc.psum_tensor(f"ps_b{j}", [128, TCW], F32)).ap())
            for j in range(NTC)]

    g_ld = SemGroup(st, nc, "s_ld", 4)
    g_st = SemGroup(st, nc, "s_st", 4)
    g_sc = SemGroup(st, nc, "s_sc", 2)
    s_g = Sem(st, nc, "s_g")
    s_t = Sem(st, nc, "s_t")
    s_v = Sem(st, nc, "s_v")
    s_a = Sem(st, nc, "s_a")

    block = st.enter_context(nc.Block())

    Buf.WAITED = {}

    class RecInstr:
        __slots__ = ("incs",)
        def __init__(self):
            self.incs = []
        def then_inc(self, sem, amt):
            self.incs.append((sem, amt))
            return self

    class RecEng:
        __slots__ = ("lst", "kind")
        def __init__(self, lst, kind):
            self.lst = lst
            self.kind = kind
        def __getattr__(self, name):
            lst = self.lst
            kind = self.kind
            def call(*a, **kw):
                ri = RecInstr()
                lst.append((name, a, kw, ri))
                trace_log.append((kind, name, a, kw, ri))
                return ri
            return call

    rec = {k: [] for k in 'tvalg'}
    trace_log = []

    _eng_of = {}

    def flush():
        _eng_of.setdefault('t', block.tensor)
        for k, emit in (('t', block.tensor), ('v', block.vector),
                        ('a', block.scalar), ('l', block.sync),
                        ('g', block.gpsimd)):
            lst = rec[k]
            if not lst:
                continue
            rec[k] = []
            def f(e, lst=lst):
                for name, a, kw, ri in lst:
                    bi = getattr(e, name)(*a, **kw)
                    for sem, amt in ri.incs:
                        bi.then_inc(sem, amt)
            emit(f)

    def T_(fn):
        Buf.CUR[0] = 't'
        fn(RecEng(rec['t'], 't'))

    def V_(fn):
        Buf.CUR[0] = 'v'
        fn(RecEng(rec['v'], 'v'))

    def A_(fn):
        Buf.CUR[0] = 'a'
        fn(RecEng(rec['a'], 'a'))

    def L_(fn):
        Buf.CUR[0] = 'l'
        fn(RecEng(rec['l'], 'l'))

    def G_(fn):
        Buf.CUR[0] = 'g'
        fn(RecEng(rec['g'], 'g'))

    import os
    _PIN = os.environ.get("KD_PIN", "0") == "1"
    _GADD = os.environ.get("KD_GADD", "1") == "1"

    def pin_act(idx):
        if not _PIN:
            return
        def f(e):
            e.add_instruction(mybir.InstLoadActFuncSet(
                name=nc.get_next_instruction_name(),
                act_func_set_id=idx, ins=[], outs=[]))
        A_(f)

    def load(buf, src_ap, dst_ap=None):
        def f(e):
            buf.acquire(e)
            bi = e.dma_start(out=dst_ap if dst_ap is not None else buf.ap,
                             in_=src_ap)
            s = g_ld.nxt()
            buf.produced(s, s.inc(bi, 16))
        L_(f)

    def loadg(buf, src_ap, dst_ap=None):
        def f(e):
            buf.acquire(e)
            bi = e.dma_start(out=dst_ap if dst_ap is not None else buf.ap,
                             in_=src_ap)
            s = g_st.nxt()
            buf.produced(s, s.inc(bi, 16))
        G_(f)

    def store(buf, dst_ap, src_ap=None, key=None):
        def f(e):
            buf.wait_ready(e)
            bi = e.dma_start(out=dst_ap,
                             in_=src_ap if src_ap is not None else buf.ap)
            s = g_st.nxt()
            tok = s.inc(bi, 16)
            buf.consumed(s, tok)
            if key is not None:
                store_toks[key] = (s, tok)
        G_(f)

    store_toks = {}

    def loads_barrier():
        tgts = [(s, s.n) for s in g_st.sems]
        def fl(e):
            for s, t in tgts:
                Buf._wait(e, s, t)
        L_(fl)
        def fg(e):
            for s, t in tgts:
                Buf._wait(e, s, t)
        G_(fg)

    load(onesc_sb, onesc)
    load(ffnb_sb, ffnb)
    load(cst_sb, cst)
    loadg(onesr_r, onesr)

    def ln_stats_and_bcast(tiles, sq_slot, st_slot):
        st1 = st_slot
        for k in range(KD):
            def fa(e, k=k):
                sq_slot.acquire(e)
                tiles[k].wait_ready(e)
                bi = e.activation(sq_slot.ap, tiles[k].ap, AF.Square)
                tok = s_a.inc(bi)
                sq_slot.produced(s_a, tok)
                tiles[k].consumed(s_a, tok)
            A_(fa)

            def ft(e, k=k):
                tiles[k].wait_ready(e)
                sq_slot.wait_ready(e)
                if k == 0:
                    ps_a.acquire(e)
                    for j in range(NTC):
                        ps_b[j].acquire(e)
                tok = None
                for j in range(NTC):
                    e.matmul(ps_a.ap[0:1, j * TCW:(j + 1) * TCW],
                             onesc_sb.ap, tiles[k].ap[:, j * TCW:(j + 1) * TCW],
                             start=(k == 0), stop=(k == KD - 1))
                    bi2 = e.matmul(ps_b[j].ap[0:1, :],
                                   onesc_sb.ap, sq_slot.ap[:, j * TCW:(j + 1) * TCW],
                                   start=(k == 0), stop=(k == KD - 1))
                    tok = s_t.inc(bi2)
                sq_slot.consumed(s_t, tok)
                tiles[k].consumed(s_t, tok)
                if k == KD - 1:
                    ps_a.produced(s_t, tok)
                    for j in range(NTC):
                        ps_b[j].produced(s_t, tok)
            T_(ft)

        def fv(e):
            ps_a.wait_ready(e)
            for j in range(NTC):
                ps_b[j].wait_ready(e)
            smu.acquire(e)
            st1.acquire(e)
            pool[12].acquire(e)
            mu2 = pool[12].ap[0:1, :]
            s1 = st1.ap[0:1, :]
            e.tensor_scalar(smu.ap, ps_a.ap[0:1, :], 1.0 / 1024.0, None, OP.mult)
            e.tensor_scalar(mu2, ps_a.ap[0:1, :], 1.0 / 1024.0, None, OP.mult)
            for j in range(NTC):
                e.tensor_scalar(s1[:, j * TCW:(j + 1) * TCW],
                                ps_b[j].ap[0:1, :], 1.0 / 1024.0, None, OP.mult)
            e.tensor_mul(mu2, mu2, mu2)
            bi = e.tensor_sub(s1, s1, mu2)
            tok = s_v.inc(bi)
            ps_a.consumed(s_v, tok)
            for j in range(NTC):
                ps_b[j].consumed(s_v, tok)
            pool[12].consumed(s_v, tok)
            smu.produced(s_v, tok)
            st1.produced(s_v, tok)
        V_(fv)

        def fa2(e):
            st1.wait_ready(e)
            cst_sb.wait_ready(e)
            s1 = st1.ap[0:1, :]
            e.activation(s1, s1, AF.Ln, bias=cst_sb.ap[0:1, 0:1])
            srr.acquire(e)
            bi = e.activation(srr.ap, s1, AF.Exp,
                              scale=cst_sb.ap[0:1, 2:3])
            tok = s_a.inc(bi)
            st1.consumed(s_a, tok)
            srr.produced(s_a, tok)
        A_(fa2)

        def ft2(e):
            ps_a.acquire(e)
            smu.wait_ready(e)
            srr.wait_ready(e)
            onesr_r.wait_ready(e)
            tok = None
            for j in range(NTC):
                ps_b[j].acquire(e)
                e.matmul(ps_a.ap[:, j * TCW:(j + 1) * TCW], onesr_r.ap,
                         smu.ap[:, j * TCW:(j + 1) * TCW], start=True, stop=True)
                bi2 = e.matmul(ps_b[j].ap, onesr_r.ap,
                               srr.ap[:, j * TCW:(j + 1) * TCW], start=True, stop=True)
                tok = s_t.inc(bi2)
            smu.consumed(s_t, tok)
            srr.consumed(s_t, tok)
            ps_a.produced(s_t, tok)
            for j in range(NTC):
                ps_b[j].produced(s_t, tok)
        T_(ft2)

    def p0_ln(b):
        for k in range(KD):
            load(pool[k], xfm[b, k])
        ln_stats_and_bcast(pool[0:KD], pool[11], pool[10])
        for k in range(KD):
            def fv3(e, k=k):
                ps_a.wait_ready(e)
                pool[10].acquire(e)
                e.tensor_sub(pool[10].ap, pool[k].ap, ps_a.ap)
                tok = None
                for j in range(NTC):
                    ps_b[j].wait_ready(e)
                    bi = e.tensor_mul(pool[k].ap[:, j * TCW:(j + 1) * TCW],
                                      pool[10].ap[:, j * TCW:(j + 1) * TCW],
                                      ps_b[j].ap)
                    tok = s_v.inc(bi)
                    ps_b[j].consumed(s_v, tok)
                ps_a.consumed(s_v, tok)
                pool[10].consumed(s_v, tok)
                pool[k].produced(s_v, tok)
            V_(fv3)
            store(pool[k], xn_d[b, k])

    def ffn_stage(b, sname, w_ap, in_d, inname, out_d, outname, is_last):
        slot_i = [0]
        for j in range(NTC):
            for k in range(KD):
                wslot = wt_f[k % 2]
                loadg(wslot, w_ap[k * 128:(k + 1) * 128, :])
                ldslot = ld_r[k % 2]
                rawkey = (inname, b, k, j)

                def fld(e, ldslot=ldslot, k=k, j=j, rawkey=rawkey):
                    ldslot.acquire(e)
                    if rawkey in store_toks:
                        ss, tt = store_toks[rawkey]
                        Buf._wait(e, ss, tt)
                    bi = e.dma_start(out=ldslot.ap[:, 0:TCW],
                                     in_=in_d[b, k, :, j * TCW:(j + 1) * TCW])
                    s = g_st.nxt()
                    ldslot.produced(s, s.inc(bi, 16))
                G_(fld)

                def ftk(e, k=k, wslot=wslot, ldslot=ldslot):
                    wslot.wait_ready(e)
                    ldslot.wait_ready(e)
                    tok = None
                    for m in range(KD):
                        if k == 0:
                            (ps_a if m < 4 else ps_b[m - 4]).acquire(e)
                        ob = (ps_a.ap[:, (m % 4) * TCW:(m % 4 + 1) * TCW]
                              if m < 4 else ps_b[m - 4].ap)
                        bi = e.matmul(ob, wslot.ap[:, m * 128:(m + 1) * 128],
                                      ldslot.ap[:, 0:TCW],
                                      start=(k == 0), stop=(k == KD - 1))
                        tok = s_t.inc(bi)
                    ldslot.consumed(s_t, tok)
                    wslot.consumed(s_t, tok)
                    if k == KD - 1:
                        ps_a.produced(s_t, tok)
                        for m in range(4):
                            ps_b[m].produced(s_t, tok)
                T_(ftk)

            for m in range(KD):
                slot = pool[8 + slot_i[0] % 3]
                slot_i[0] += 1

                def fe(e, m=m, slot=slot):
                    slot.acquire(e)
                    src = ps_a if m < 4 else ps_b[m - 4]
                    src.wait_ready(e)
                    sap = (ps_a.ap[:, (m % 4) * TCW:(m % 4 + 1) * TCW]
                           if m < 4 else ps_b[m - 4].ap)
                    if is_last:
                        ffnb_sb.wait_ready(e)
                        bi = e.activation(slot.ap[:, 0:TCW], sap, AF.Tanh,
                                          bias=ffnb_sb.ap[:, m:m + 1])
                        tok = s_a.inc(bi)
                        ffnb_sb.consumed(s_a, tok)
                    else:
                        bi = e.activation(slot.ap[:, 0:TCW], sap, AF.Copy)
                        tok = s_a.inc(bi)
                    src.consumed(s_a, tok)
                    slot.produced(s_a, tok)
                A_(fe)
                store(slot, out_d[b, m, :, j * TCW:(j + 1) * TCW],
                      src_ap=slot.ap[:, 0:TCW], key=(outname, b, m, j))

    def ffn(b):
        loads_barrier()
        ffn_stage(b, "s1", ffnw, xn_d, "xn", h1_d, "h1", False)
        flush()
        ffn_stage(b, "s2", ffna1, h1_d, "h1", h2_d, "h2", False)
        flush()
        ffn_stage(b, "s3", ffna2, h2_d, "h2", ffn_d, "ffn", True)

    def load_layer_weights(l):
        loadg(xpwT_sb, W[f"xpwT{l}"])
        load(convw_sb, W[f"convw{l}"])
        load(convb_sb, W[f"convb{l}"])
        load(dtb_sb, W[f"dtb{l}"])
        load(dsk_sb, W[f"dsk{l}"])
        load(lnw_sb, W[f"lnw{l}"])
        load(lnb_sb, W[f"lnb{l}"])
        load(A_sb, W[f"alog{l}"])

        def fa(e):
            A_sb.wait_ready(e)
            bi = e.activation(A_sb.ap, A_sb.ap, AF.Exp)
            A_sb.produced(s_a, s_a.inc(bi))
        A_(fa)

        def fv(e):
            A_sb.wait_ready(e)
            bi = e.tensor_scalar(A_sb.ap, A_sb.ap, -1.0, None, OP.mult)
            A_sb.produced(s_v, s_v.inc(bi))
        V_(fv)

    def mamba_A(l, b, stream_d):
        loads_barrier()
        for k in range(KD):
            load(pool[k], stream_d[b, k])
        for m in range(2 * KI):
            dtile = m % KI
            is_z = m >= KI

            def fw(e, m=m):
                wt_in.acquire(e)
                bi = e.dma_start(out=wt_in.ap, in_=W[f"inwT{l}"][m])
                s = g_ld.nxt()
                wt_in.produced(s, s.inc(bi, 16))
            L_(fw)

            def ft(e, m=m):
                wt_in.wait_ready(e)
                for k in range(KD):
                    pool[k].wait_ready(e)
                tok = None
                for j in range(NTC):
                    ps_b[j].acquire(e)
                    for k in range(KD):
                        bi = e.matmul(ps_b[j].ap,
                                      wt_in.ap[:, k * 128:(k + 1) * 128],
                                      pool[k].ap[:, j * TCW:(j + 1) * TCW],
                                      start=(k == 0), stop=(k == KD - 1))
                        tok = s_t.inc(bi)
                    ps_b[j].produced(s_t, tok)
                wt_in.consumed(s_t, tok)
                for k in range(KD):
                    pool[k].consumed(s_t, tok)
            T_(ft)

            if not is_z:
                def fe(e):
                    conv_sb.acquire(e)
                    tok = None
                    for j in range(NTC):
                        ps_b[j].wait_ready(e)
                        bi = e.activation(
                            conv_sb.ap[:, j * TCW:(j + 1) * TCW],
                            ps_b[j].ap, AF.Copy)
                        tok = s_a.inc(bi)
                        ps_b[j].consumed(s_a, tok)
                    conv_sb.produced(s_a, tok)
                A_(fe)

                def fcv(e, dtile=dtile):
                    conv_sb.wait_ready(e)
                    convw_sb.wait_ready(e)
                    pool[9].acquire(e)
                    e.tensor_scalar(pool[9].ap, conv_sb.ap[:, 0:T],
                                    convw_sb.ap[:, dtile * 4 + 3:dtile * 4 + 4],
                                    None, OP.mult)
                    bi = None
                    for tap in (0, 1, 2):
                        sh = 3 - tap
                        bi = e.scalar_tensor_tensor(
                            pool[9].ap[:, sh:T], conv_sb.ap[:, 0:T - sh],
                            convw_sb.ap[:, dtile * 4 + tap:dtile * 4 + tap + 1],
                            pool[9].ap[:, sh:T], OP.mult, OP.add)
                    tok = s_v.inc(bi)
                    conv_sb.consumed(s_v, tok)
                    convw_sb.consumed(s_v, tok)
                    pool[9].produced(s_v, tok)
                V_(fcv)

                def fu(e, dtile=dtile):
                    pool[9].wait_ready(e)
                    convb_sb.wait_ready(e)
                    pool[10].acquire(e)
                    bi = e.activation(pool[10].ap, pool[9].ap, AF.Silu,
                                      bias=convb_sb.ap[:, dtile:dtile + 1])
                    tok = s_a.inc(bi)
                    pool[9].consumed(s_a, tok)
                    convb_sb.consumed(s_a, tok)
                    pool[10].produced(s_a, tok)
                A_(fu)
                store(pool[10], u_d[b, dtile])

                def fur(e, dtile=dtile):
                    pool[9].wait_ready(e)
                    u_r.acquire(e)
                    bi = e.activation(u_r.ap, pool[9].ap, AF.Silu,
                                      bias=convb_sb.ap[:, dtile:dtile + 1])
                    tok = s_a.inc(bi)
                    pool[9].consumed(s_a, tok)
                    u_r.produced(s_a, tok)
                A_(fur)

                def fxp(e, dtile=dtile):
                    u_r.wait_ready(e)
                    xpwT_sb.wait_ready(e)
                    if dtile == 0:
                        ps_a.acquire(e)
                    tok = None
                    for j in range(NTC):
                        bi = e.matmul(ps_a.ap[0:96, j * TCW:(j + 1) * TCW],
                                      xpwT_sb.ap[:, dtile * 96:(dtile + 1) * 96],
                                      u_r.ap[:, j * TCW:(j + 1) * TCW],
                                      start=(dtile == 0), stop=(dtile == KI - 1))
                        tok = s_t.inc(bi)
                    u_r.consumed(s_t, tok)
                    xpwT_sb.consumed(s_t, tok)
                    if dtile == KI - 1:
                        ps_a.produced(s_t, tok)
                T_(fxp)
            else:
                def fz(e, dtile=dtile):
                    pool[10].acquire(e)
                    tok = None
                    for j in range(NTC):
                        ps_b[j].wait_ready(e)
                        bi = e.activation(pool[10].ap[:, j * TCW:(j + 1) * TCW],
                                          ps_b[j].ap, AF.Silu)
                        tok = s_a.inc(bi)
                        ps_b[j].consumed(s_a, tok)
                    pool[10].produced(s_a, tok)
                A_(fz)
                store(pool[10], zs_d[b, dtile])

        def fxe(e, b=b):
            ps_a.wait_ready(e)
            xdbl[b].acquire(e)
            bi = e.activation(xdbl[b].ap, ps_a.ap[0:96, :], AF.Copy)
            tok = s_a.inc(bi)
            ps_a.consumed(s_a, tok)
            xdbl[b].produced(s_a, tok)
        A_(fxe)

    def mamba_scan(l, b):
        loads_barrier()
        for g in range(NGRP):
            for i in range(G):
                dtile = g * G + i

                loadg(dtw1, W[f"dtwT{l}"][:, dtile * 128:(dtile + 1) * 128])

                def fdt(e, dtile=dtile):
                    dtw1.wait_ready(e)
                    xdbl[b].wait_ready(e)
                    tok = None
                    for j in range(NTC):
                        ps_b[j].acquire(e)
                        bi = e.matmul(ps_b[j].ap,
                                      dtw1.ap,
                                      xdbl[b].ap[0:64, j * TCW:(j + 1) * TCW],
                                      start=True, stop=True)
                        tok = s_t.inc(bi)
                        ps_b[j].produced(s_t, tok)
                    xdbl[b].consumed(s_t, tok)
                    dtw1.consumed(s_t, tok)
                T_(fdt)

                def fsp(e, i=i, dtile=dtile):
                    pool[9].acquire(e)
                    dtb_sb.wait_ready(e)
                    tok = None
                    for j in range(NTC):
                        ps_b[j].wait_ready(e)
                        bi = e.activation(pool[9].ap[:, j * TCW:(j + 1) * TCW],
                                          ps_b[j].ap, AF.Exp,
                                          bias=dtb_sb.ap[:, dtile:dtile + 1])
                        tok = s_a.inc(bi)
                        ps_b[j].consumed(s_a, tok)
                    dtb_sb.consumed(s_a, tok)
                    pool[4 + i].acquire(e)
                    bi = e.activation(pool[4 + i].ap, pool[9].ap, AF.Ln, bias=cst_sb.ap[:, 1:2])
                    tok = s_a.inc(bi)
                    pool[9].consumed(s_a, tok)
                    pool[4 + i].produced(s_a, tok)
                A_(fsp)

                load(pool[0 + i], u_d[b, dtile])

                def fdtu(e, i=i):
                    pool[4 + i].wait_ready(e)
                    pool[0 + i].wait_ready(e)
                    pool[6 + i].acquire(e)
                    bi = e.tensor_mul(pool[6 + i].ap, pool[4 + i].ap, pool[0 + i].ap)
                    tok = s_v.inc(bi)
                    pool[4 + i].consumed(s_v, tok)
                    pool[0 + i].consumed(s_v, tok)
                    pool[6 + i].produced(s_v, tok)
                V_(fdtu)

            for n in range(NST):
                def fstg(e, n=n, b=b):
                    stgB.acquire(e)
                    stgC.acquire(e)
                    xdbl[b].wait_ready(e)
                    s1 = g_sc.nxt()
                    bi = e.dma_start(out=stgB.ap, in_=xdbl[b].ap[64 + n:65 + n, :])
                    t1 = s1.inc(bi, 16)
                    s2 = g_sc.nxt()
                    bi = e.dma_start(out=stgC.ap, in_=xdbl[b].ap[80 + n:81 + n, :])
                    t2 = s2.inc(bi, 16)
                    stgB.produced(s1, t1)
                    stgC.produced(s2, t2)
                    xdbl[b].consumed(s2, t2)
                A_(fstg)

                for half, stg, dstb in ((0, stgB, ld_r[0]), (1, stgC, ld_r[1])):
                    def fbc(e, stg=stg):
                        stg.wait_ready(e)
                        onesr_r.wait_ready(e)
                        tok = None
                        for j in range(NTC):
                            ps_b[j].acquire(e)
                            bi = e.matmul(ps_b[j].ap, onesr_r.ap,
                                          stg.ap[:, j * TCW:(j + 1) * TCW],
                                          start=True, stop=True)
                            tok = s_t.inc(bi)
                            ps_b[j].produced(s_t, tok)
                        stg.consumed(s_t, tok)
                    T_(fbc)

                    def fev(e, dstb=dstb):
                        dstb.acquire(e)
                        tok = None
                        for j in range(NTC):
                            ps_b[j].wait_ready(e)
                            bi = e.activation(dstb.ap[:, j * TCW:(j + 1) * TCW],
                                              ps_b[j].ap, AF.Copy)
                            tok = s_a.inc(bi)
                            ps_b[j].consumed(s_a, tok)
                        dstb.produced(s_a, tok)
                    A_(fev)

                for i in range(G):
                    dtile = g * G + i
                    ci = dtile * NST + n
                    da = pool[8] if (n * G + i) % 2 == 0 else pool[12]

                    def fda(e, i=i, ci=ci, da=da):
                        da.acquire(e)
                        pool[4 + i].wait_ready(e)
                        A_sb.wait_ready(e)
                        bi = e.activation(da.ap, pool[4 + i].ap, AF.Exp,
                                          scale=A_sb.ap[:, ci:ci + 1])
                        tok = s_a.inc(bi)
                        pool[4 + i].consumed(s_a, tok)
                        A_sb.consumed(s_a, tok)
                        da.produced(s_a, tok)
                    A_(fda)

                    def fscan(e, i=i, n=n, da=da):
                        pool[9].acquire(e)
                        ld_r[0].wait_ready(e)
                        pool[6 + i].wait_ready(e)
                        Bb32 = ld_r[0].ap.bitcast(F32)
                        bi = e.tensor_mul(pool[9].ap, pool[6 + i].ap, Bb32)
                        tok = s_v.inc(bi)
                        ld_r[0].consumed(s_v, tok)
                        pool[6 + i].consumed(s_v, tok)
                        pool[9].produced(s_v, tok)
                        da.wait_ready(e)
                        pool[10].acquire(e)
                        bi = e.tensor_tensor_scan(pool[10].ap, da.ap,
                                                  pool[9].ap, 0.0,
                                                  OP.mult, OP.add)
                        tok = s_v.inc(bi)
                        da.consumed(s_v, tok)
                        pool[9].consumed(s_v, tok)
                        pool[10].produced(s_v, tok)
                        pool[9].acquire(e)
                        if n == 0:
                            pool[2 + i].acquire(e)
                        ld_r[1].wait_ready(e)
                        Cb32 = ld_r[1].ap.bitcast(F32)
                        dst = (pool[2 + i].ap if n == 0 else pool[9].ap)
                        bi = e.tensor_mul(dst, pool[10].ap, Cb32)
                        tok = s_v.inc(bi)
                        ld_r[1].consumed(s_v, tok)
                        pool[10].consumed(s_v, tok)
                        if n == 0:
                            pool[2 + i].produced(s_v, tok)
                        else:
                            pool[9].produced(s_v, tok)
                    V_(fscan)

                    if n > 0:
                        def fadd(e, i=i):
                            pool[9].wait_ready(e)
                            pool[2 + i].wait_ready(e)
                            bi = e.tensor_add(pool[2 + i].ap, pool[2 + i].ap,
                                              pool[9].ap)
                            if _GADD:
                                tok = s_g.inc(bi)
                                pool[9].consumed(s_g, tok)
                                pool[2 + i].produced(s_g, tok)
                            else:
                                tok = s_v.inc(bi)
                                pool[9].consumed(s_v, tok)
                                pool[2 + i].produced(s_v, tok)
                        if _GADD:
                            G_(fadd)
                        else:
                            V_(fadd)

            flush()
            for i in range(G):
                dtile = g * G + i
                load(pool[11], zs_d[b, dtile])

                def fg(e, i=i, dtile=dtile):
                    pool[8].acquire(e)
                    pool[0 + i].wait_ready(e)
                    pool[2 + i].wait_ready(e)
                    dsk_sb.wait_ready(e)
                    bi = e.scalar_tensor_tensor(
                        pool[8].ap, pool[0 + i].ap,
                        dsk_sb.ap[:, dtile:dtile + 1], pool[2 + i].ap,
                        OP.mult, OP.add)
                    tok = s_v.inc(bi)
                    pool[0 + i].consumed(s_v, tok)
                    pool[2 + i].consumed(s_v, tok)
                    dsk_sb.consumed(s_v, tok)
                    pool[8].produced(s_v, tok)
                    pool[11].wait_ready(e)
                    pool[10].acquire(e)
                    bi = e.tensor_mul(pool[10].ap, pool[8].ap, pool[11].ap)
                    tok = s_v.inc(bi)
                    pool[8].consumed(s_v, tok)
                    pool[11].consumed(s_v, tok)
                    pool[10].produced(s_v, tok)
                V_(fg)
                store(pool[10], yp_d[b, dtile])

    def mamba_out(l, b, stream_d, next_d):
        loads_barrier()
        for mp in range(4):
            def fw(e, mp=mp):
                for mi in range(2):
                    m = mp * 2 + mi
                    wt_out[mi].acquire(e)
                    bi = e.dma_start(out=wt_out[mi].ap, in_=W[f"outwT{l}"][m])
                    s = g_st.nxt()
                    wt_out[mi].produced(s, s.inc(bi, 16))
            G_(fw)

            for k in range(KI):
                ldslot = ld_r[k % 2]
                loadg(ldslot, yp_d[b, k])

                def ftk(e, k=k, ldslot=ldslot):
                    wt_out[0].wait_ready(e)
                    wt_out[1].wait_ready(e)
                    ldslot.wait_ready(e)
                    tok = None
                    for mi in range(2):
                        for j in range(NTC):
                            if k == 0:
                                (ps_a if mi == 0 else ps_b[j]).acquire(e)
                            ob = (ps_a.ap[:, j * TCW:(j + 1) * TCW] if mi == 0
                                  else ps_b[j].ap)
                            bi = e.matmul(ob,
                                          wt_out[mi].ap[:, k * 128:(k + 1) * 128],
                                          ldslot.ap[:, j * TCW:(j + 1) * TCW],
                                          start=(k == 0), stop=(k == KI - 1))
                            tok = s_t.inc(bi)
                    ldslot.consumed(s_t, tok)
                    wt_out[0].consumed(s_t, tok)
                    wt_out[1].consumed(s_t, tok)
                    if k == KI - 1:
                        ps_a.produced(s_t, tok)
                        for j in range(NTC):
                            ps_b[j].produced(s_t, tok)
                T_(ftk)

            for mi in range(2):
                m = mp * 2 + mi

                def fe(e, m=m, mi=mi):
                    pool[m].acquire(e)
                    tok = None
                    if mi == 0:
                        ps_a.wait_ready(e)
                        bi = e.activation(pool[m].ap, ps_a.ap, AF.Copy)
                        tok = s_a.inc(bi)
                        ps_a.consumed(s_a, tok)
                    else:
                        for j in range(NTC):
                            ps_b[j].wait_ready(e)
                            bi = e.activation(pool[m].ap[:, j * TCW:(j + 1) * TCW],
                                              ps_b[j].ap, AF.Copy)
                            tok = s_a.inc(bi)
                            ps_b[j].consumed(s_a, tok)
                    pool[m].produced(s_a, tok)
                A_(fe)

        ln_stats_and_bcast(pool[0:KD], pool[10], pool[11])
        for k in range(KD):
            load(pool[11], stream_d[b, k])

            def fv3(e, k=k):
                ps_a.wait_ready(e)
                pool[10].acquire(e)
                e.tensor_sub(pool[10].ap, pool[k].ap, ps_a.ap)
                tok = None
                for j in range(NTC):
                    ps_b[j].wait_ready(e)
                    bi = e.tensor_mul(pool[10].ap[:, j * TCW:(j + 1) * TCW],
                                      pool[10].ap[:, j * TCW:(j + 1) * TCW],
                                      ps_b[j].ap)
                    tok = s_v.inc(bi)
                    ps_b[j].consumed(s_v, tok)
                ps_a.consumed(s_v, tok)
                pool[k].consumed(s_v, tok)
                lnw_sb.wait_ready(e)
                lnb_sb.wait_ready(e)
                e.tensor_scalar(pool[10].ap, pool[10].ap,
                                lnw_sb.ap[:, k:k + 1], lnb_sb.ap[:, k:k + 1],
                                OP.mult, OP.add)
                pool[11].wait_ready(e)
                bi = e.tensor_add(pool[10].ap, pool[10].ap, pool[11].ap)
                tok = s_v.inc(bi)
                lnw_sb.consumed(s_v, tok)
                lnb_sb.consumed(s_v, tok)
                pool[11].consumed(s_v, tok)
                pool[10].produced(s_v, tok)
            V_(fv3)
            store(pool[10], next_d[b, k])

    def final(b):
        loads_barrier()
        for k in range(KD):
            base = (k % 2) * 4
            i0, i1, i2, i3 = base, base + 1, base + 2, base + 3
            load(pool[i0], xfm[b, k])
            load(pool[i1], m2_d[b, k])
            load(pool[i2], ffn_d[b, k])

            def fv(e, i0=i0, i1=i1, i2=i2, i3=i3):
                pool[i0].wait_ready(e)
                pool[i1].wait_ready(e)
                pool[i2].wait_ready(e)
                pool[i3].acquire(e)
                e.tensor_add(pool[i3].ap, pool[i0].ap, pool[i1].ap)
                bi = e.tensor_add(pool[i3].ap, pool[i3].ap, pool[i2].ap)
                tok = s_v.inc(bi)
                pool[i0].consumed(s_v, tok)
                pool[i1].consumed(s_v, tok)
                pool[i2].consumed(s_v, tok)
                pool[i3].produced(s_v, tok)
            V_(fv)
            store(pool[i3], outfm[b, k])

    for b in range(BC):
        p0_ln(b)
        flush()
        ffn(b)
        flush()
    for l in (0, 1):
        load_layer_weights(l)
        flush()
        for b in range(BC):
            mamba_A(l, b, xn_d if l == 0 else m1_d)
            flush()
            mamba_scan(l, b)
            mamba_out(l, b, xn_d if l == 0 else m1_d, m1_d if l == 0 else m2_d)
            flush()
    for b in range(BC):
        final(b)
    flush()

    def f_fin_g(e):
        for s in g_st.sems:
            e.wait_ge(s.h, s.n)
    G_(f_fin_g)
    def f_fin_l(e):
        for s in g_ld.sems:
            e.wait_ge(s.h, s.n)
    L_(f_fin_l)
    def f_fin_a(e):
        for s in g_sc.sems:
            e.wait_ge(s.h, s.n)
    A_(f_fin_a)

    flush()

    st.close()
    nc._kd_trace = trace_log
    return nc


def host_inputs(inputs, core):
    asc = np.ascontiguousarray
    x = np.asarray(inputs["L0_attn_x"])[2 * core:2 * core + 2]
    d = {
        "xfm": asc(x.transpose(0, 2, 1)).reshape(BC, KD, 128, T),
        "ffnw": asc(np.asarray(inputs["L0_ffn_weight"])),
        "ffna1": asc(np.asarray(inputs["L0_ffn_aux_weight1"])),
        "ffna2": asc(np.asarray(inputs["L0_ffn_aux_weight2"])),
        "ffnb": asc(np.asarray(inputs["L0_ffn_bias"]).reshape(8, 128).T),
        "onesr": np.ones((1, 128), np.float32),
        "onesc": np.ones((128, 1), np.float32),
        "cst": np.repeat(np.array([[EPS, 1.0, -0.5, 0.0]], np.float32), 128, axis=0),
    }
    for l in (0, 1):
        inwT = np.asarray(inputs["L0_attn_in_proj_w"])[l].T
        d[f"inwT{l}"] = asc(inwT.reshape(8, 128, 32, 128)
                            .transpose(2, 1, 0, 3)).reshape(32, 128, 8 * 128)
        outwT = np.asarray(inputs["L0_attn_out_proj_w"])[l].T
        d[f"outwT{l}"] = asc(outwT.reshape(16, 128, 8, 128)
                             .transpose(2, 1, 0, 3)).reshape(8, 128, 16 * 128)
        d[f"dtwT{l}"] = asc(np.asarray(inputs["L0_attn_dt_proj_w"])[l].T)
        d[f"xpwT{l}"] = asc(np.asarray(inputs["L0_attn_x_proj_w"])[l].T
                            .reshape(16, 128, 96).transpose(1, 0, 2)).reshape(128, 16 * 96)
        d[f"convw{l}"] = asc(np.asarray(inputs["L0_attn_conv_w"])[l]
                             .reshape(16, 128, 4).transpose(1, 0, 2)).reshape(128, 64)
        d[f"convb{l}"] = asc(np.asarray(inputs["L0_attn_conv_b"])[l].reshape(16, 128).T)
        d[f"dtb{l}"] = asc(np.asarray(inputs["L0_attn_dt_proj_b"])[l].reshape(16, 128).T)
        d[f"dsk{l}"] = asc(np.asarray(inputs["L0_attn_Dskip"])[l].reshape(16, 128).T)
        d[f"alog{l}"] = asc(np.asarray(inputs["L0_attn_A_log"])[l]
                            .reshape(16, 128, 16).transpose(1, 0, 2)).reshape(128, 256)
        d[f"lnw{l}"] = asc(np.asarray(inputs["L0_attn_ln_w"])[l].reshape(8, 128).T)
        d[f"lnb{l}"] = asc(np.asarray(inputs["L0_attn_ln_b"])[l].reshape(8, 128).T)
    return {k: np.ascontiguousarray(v, dtype=np.float32) for k, v in d.items()}


def kernel(**inputs):
    nc = build_nc()
    in_maps = [host_inputs(inputs, c) for c in range(NCORE)]
    res = run_bass_kernel_spmd(nc, in_maps, core_ids=list(range(NCORE)))
    outs = []
    for c in range(NCORE):
        o = np.asarray(res.results[c]["outfm"])
        o = o.reshape(BC, 1024, T).transpose(0, 2, 1)
        outs.append(o)
    return np.ascontiguousarray(np.concatenate(outs, axis=0), dtype=np.float32)


# version 06
# speedup vs baseline: 1.0383x; 1.0383x over previous
import numpy as np
from contextlib import ExitStack
import concourse.bass as bass
import concourse.mybir as mybir
from concourse.bass_utils import run_bass_kernel_spmd

F32 = mybir.dt.float32
F32R = mybir.dt.float32r
AF = mybir.ActivationFunctionType
OP = mybir.AluOpType

NCORE = 8
BC = 2
T = 2048
KD = 8
KI = 16
NST = 16
TCW = 512
NTC = T // TCW
G = 2
NGRP = KI // G
EPS = 1e-5


class Sem:
    def __init__(self, st, nc, name):
        self.h = st.enter_context(nc.semaphore(name))
        self.n = 0

    def inc(self, bi, amt=1):
        bi.then_inc(self.h, amt)
        self.n += amt
        return self.n


class SemGroup:
    def __init__(self, st, nc, name, k):
        self.sems = [Sem(st, nc, f"{name}{i}") for i in range(k)]
        self.i = 0

    def nxt(self):
        s = self.sems[self.i]
        self.i = (self.i + 1) % len(self.sems)
        return s


class Buf:
    CUR = ['?']
    WAITED = {}

    @classmethod
    def _wait(cls, eng, s, t):
        key = (cls.CUR[0], id(s))
        if cls.WAITED.get(key, -1) >= t:
            return
        eng.wait_ge(s.h, t)
        cls.WAITED[key] = t

    def __init__(self, ap):
        self.ap = ap
        self.ready = None
        self.cons = {}

    def acquire(self, eng):
        for s, t in self.cons.items():
            Buf._wait(eng, s, t)
        self.cons = {}

    def produced(self, s, tok):
        self.ready = (s, tok)

    def wait_ready(self, eng):
        if self.ready is not None:
            s, t = self.ready
            Buf._wait(eng, s, t)

    def consumed(self, s, tok):
        if s in self.cons:
            self.cons[s] = max(self.cons[s], tok)
        else:
            self.cons[s] = tok


def build_nc():
    nc = bass.Bass("TRN2", target_bir_lowering=False)

    def din(name, shape):
        return nc.dram_tensor(name, shape, F32, kind="ExternalInput").ap()

    def dout(name, shape):
        return nc.dram_tensor(name, shape, F32, kind="ExternalOutput").ap()

    def dint(name, shape):
        return nc.dram_tensor(name, shape, F32, kind="Internal").ap()

    xfm = din("xfm", [BC, KD, 128, T])
    W = {}
    for l in (0, 1):
        W[f"inwT{l}"] = din(f"inwT{l}", [32, 128, 8 * 128])
        W[f"outwT{l}"] = din(f"outwT{l}", [8, 128, 16 * 128])
        W[f"dtwT{l}"] = din(f"dtwT{l}", [64, 2048])
        W[f"xpwT{l}"] = din(f"xpwT{l}", [128, 16 * 96])
        W[f"convw{l}"] = din(f"convw{l}", [128, 64])
        W[f"convb{l}"] = din(f"convb{l}", [128, 16])
        W[f"dtb{l}"] = din(f"dtb{l}", [128, 16])
        W[f"dsk{l}"] = din(f"dsk{l}", [128, 16])
        W[f"alog{l}"] = din(f"alog{l}", [128, 256])
        W[f"lnw{l}"] = din(f"lnw{l}", [128, 8])
        W[f"lnb{l}"] = din(f"lnb{l}", [128, 8])
    ffnw = din("ffnw", [1024, 1024])
    ffna1 = din("ffna1", [1024, 1024])
    ffna2 = din("ffna2", [1024, 1024])
    ffnb = din("ffnb", [128, 8])
    onesr = din("onesr", [1, 128])
    onesc = din("onesc", [128, 1])
    cst = din("cst", [128, 4])

    outfm = dout("outfm", [BC, KD, 128, T])

    xn_d = dint("xn_d", [BC, KD, 128, T])
    ffn_d = dint("ffn_d", [BC, KD, 128, T])
    h1_d = dint("h1_d", [BC, KD, 128, T])
    h2_d = dint("h2_d", [BC, KD, 128, T])
    m1_d = dint("m1_d", [BC, KD, 128, T])
    m2_d = dint("m2_d", [BC, KD, 128, T])
    u_d = dint("u_d", [BC, KI, 128, T])
    zs_d = dint("zs_d", [BC, KI, 128, T])
    yp_d = dint("yp_d", [BC, KI, 128, T])

    st = ExitStack()
    st.__enter__()

    def mksb(name, shape, dt=F32):
        return Buf(st.enter_context(nc.sbuf_tensor(name, shape, dt)).ap())

    NPOOL = 13
    pool = [mksb(f"pl{i}", [128, T]) for i in range(NPOOL)]
    conv_sb = pool[11]
    xdbl = [mksb(f"xdbl{b}", [96, T], F32R) for b in range(BC)]
    wt_in = mksb("wt_in", [128, 1024])
    wt_f = [mksb("wt_f0", [128, 1024], F32R), mksb("wt_f1", [128, 1024], F32R)]
    ld_r = [mksb("ld_r0", [128, T], F32R), mksb("ld_r1", [128, T], F32R)]
    wt_out = [mksb("wt_out0", [128, 2048], F32R), mksb("wt_out1", [128, 2048], F32R)]
    xpwT_sb = mksb("xpwT_sb", [128, 1536], F32R)
    dtw1 = mksb("dtw1", [64, 128], F32R)
    u_r = mksb("u_r", [128, T], F32R)
    onesr_r = mksb("onesr_r", [1, 128], F32R)
    A_sb = mksb("A_sb", [128, 256])
    convw_sb = mksb("convw_sb", [128, 64])
    convb_sb = mksb("convb_sb", [128, 16])
    dtb_sb = mksb("dtb_sb", [128, 16])
    dsk_sb = mksb("dsk_sb", [128, 16])
    lnw_sb = mksb("lnw_sb", [128, 8])
    lnb_sb = mksb("lnb_sb", [128, 8])
    ffnb_sb = mksb("ffnb_sb", [128, 8])
    onesc_sb = mksb("onesc_sb", [128, 1])
    cst_sb = mksb("cst_sb", [128, 4])
    stgB = mksb("stgB", [1, T], F32R)
    stgC = mksb("stgC", [1, T], F32R)
    smu = mksb("smu", [1, T], F32R)
    srr = stgC

    ps_a = Buf(st.enter_context(nc.psum_tensor("ps_a", [128, T], F32)).ap())
    ps_b = [Buf(st.enter_context(nc.psum_tensor(f"ps_b{j}", [128, TCW], F32)).ap())
            for j in range(NTC)]

    g_ld = SemGroup(st, nc, "s_ld", 4)
    g_st = SemGroup(st, nc, "s_st", 4)
    g_sc = SemGroup(st, nc, "s_sc", 2)
    s_g = Sem(st, nc, "s_g")
    s_t = Sem(st, nc, "s_t")
    s_v = Sem(st, nc, "s_v")
    s_a = Sem(st, nc, "s_a")

    block = st.enter_context(nc.Block())

    Buf.WAITED = {}

    class RecInstr:
        __slots__ = ("incs",)
        def __init__(self):
            self.incs = []
        def then_inc(self, sem, amt):
            self.incs.append((sem, amt))
            return self

    class RecEng:
        __slots__ = ("lst", "kind")
        def __init__(self, lst, kind):
            self.lst = lst
            self.kind = kind
        def __getattr__(self, name):
            lst = self.lst
            kind = self.kind
            def call(*a, **kw):
                ri = RecInstr()
                lst.append((name, a, kw, ri))
                trace_log.append((kind, name, a, kw, ri))
                return ri
            return call

    rec = {k: [] for k in 'tvalg'}
    trace_log = []

    _eng_of = {}

    def flush():
        _eng_of.setdefault('t', block.tensor)
        for k, emit in (('t', block.tensor), ('v', block.vector),
                        ('a', block.scalar), ('l', block.sync),
                        ('g', block.gpsimd)):
            lst = rec[k]
            if not lst:
                continue
            rec[k] = []
            def f(e, lst=lst):
                for name, a, kw, ri in lst:
                    bi = getattr(e, name)(*a, **kw)
                    for sem, amt in ri.incs:
                        bi.then_inc(sem, amt)
            emit(f)

    def T_(fn):
        Buf.CUR[0] = 't'
        fn(RecEng(rec['t'], 't'))

    def V_(fn):
        Buf.CUR[0] = 'v'
        fn(RecEng(rec['v'], 'v'))

    def A_(fn):
        Buf.CUR[0] = 'a'
        fn(RecEng(rec['a'], 'a'))

    def L_(fn):
        Buf.CUR[0] = 'l'
        fn(RecEng(rec['l'], 'l'))

    def G_(fn):
        Buf.CUR[0] = 'g'
        fn(RecEng(rec['g'], 'g'))

    import os
    _PIN = os.environ.get("KD_PIN", "0") == "1"
    _GADD = os.environ.get("KD_GADD", "1") == "1"

    def pin_act(idx):
        if not _PIN:
            return
        def f(e):
            e.add_instruction(mybir.InstLoadActFuncSet(
                name=nc.get_next_instruction_name(),
                act_func_set_id=idx, ins=[], outs=[]))
        A_(f)

    def load(buf, src_ap, dst_ap=None):
        def f(e):
            buf.acquire(e)
            bi = e.dma_start(out=dst_ap if dst_ap is not None else buf.ap,
                             in_=src_ap)
            s = g_ld.nxt()
            buf.produced(s, s.inc(bi, 16))
        L_(f)

    def loadg(buf, src_ap, dst_ap=None):
        def f(e):
            buf.acquire(e)
            bi = e.dma_start(out=dst_ap if dst_ap is not None else buf.ap,
                             in_=src_ap)
            s = g_st.nxt()
            buf.produced(s, s.inc(bi, 16))
        G_(f)

    def store(buf, dst_ap, src_ap=None, key=None):
        def f(e):
            buf.wait_ready(e)
            bi = e.dma_start(out=dst_ap,
                             in_=src_ap if src_ap is not None else buf.ap)
            s = g_st.nxt()
            tok = s.inc(bi, 16)
            buf.consumed(s, tok)
            if key is not None:
                store_toks[key] = (s, tok)
        G_(f)

    store_toks = {}

    def loads_barrier():
        tgts = [(s, s.n) for s in g_st.sems]
        def fl(e):
            for s, t in tgts:
                Buf._wait(e, s, t)
        L_(fl)
        def fg(e):
            for s, t in tgts:
                Buf._wait(e, s, t)
        G_(fg)

    load(onesc_sb, onesc)
    load(ffnb_sb, ffnb)
    load(cst_sb, cst)
    loadg(onesr_r, onesr)

    def ln_stats_and_bcast(tiles, sq_slot, st_slot):
        st1 = st_slot
        for k in range(KD):
            def fa(e, k=k):
                sq_slot.acquire(e)
                tiles[k].wait_ready(e)
                bi = e.activation(sq_slot.ap, tiles[k].ap, AF.Square)
                tok = s_a.inc(bi)
                sq_slot.produced(s_a, tok)
                tiles[k].consumed(s_a, tok)
            A_(fa)

            def ft(e, k=k):
                tiles[k].wait_ready(e)
                sq_slot.wait_ready(e)
                if k == 0:
                    ps_a.acquire(e)
                    for j in range(NTC):
                        ps_b[j].acquire(e)
                tok = None
                for j in range(NTC):
                    e.matmul(ps_a.ap[0:1, j * TCW:(j + 1) * TCW],
                             onesc_sb.ap, tiles[k].ap[:, j * TCW:(j + 1) * TCW],
                             start=(k == 0), stop=(k == KD - 1))
                    bi2 = e.matmul(ps_b[j].ap[0:1, :],
                                   onesc_sb.ap, sq_slot.ap[:, j * TCW:(j + 1) * TCW],
                                   start=(k == 0), stop=(k == KD - 1))
                    tok = s_t.inc(bi2)
                sq_slot.consumed(s_t, tok)
                tiles[k].consumed(s_t, tok)
                if k == KD - 1:
                    ps_a.produced(s_t, tok)
                    for j in range(NTC):
                        ps_b[j].produced(s_t, tok)
            T_(ft)

        def fv(e):
            ps_a.wait_ready(e)
            for j in range(NTC):
                ps_b[j].wait_ready(e)
            smu.acquire(e)
            st1.acquire(e)
            pool[12].acquire(e)
            mu2 = pool[12].ap[0:1, :]
            s1 = st1.ap[0:1, :]
            e.tensor_scalar(smu.ap, ps_a.ap[0:1, :], 1.0 / 1024.0, None, OP.mult)
            e.tensor_scalar(mu2, ps_a.ap[0:1, :], 1.0 / 1024.0, None, OP.mult)
            for j in range(NTC):
                e.tensor_scalar(s1[:, j * TCW:(j + 1) * TCW],
                                ps_b[j].ap[0:1, :], 1.0 / 1024.0, None, OP.mult)
            e.tensor_mul(mu2, mu2, mu2)
            bi = e.tensor_sub(s1, s1, mu2)
            tok = s_v.inc(bi)
            ps_a.consumed(s_v, tok)
            for j in range(NTC):
                ps_b[j].consumed(s_v, tok)
            pool[12].consumed(s_v, tok)
            smu.produced(s_v, tok)
            st1.produced(s_v, tok)
        V_(fv)

        def fa2(e):
            st1.wait_ready(e)
            cst_sb.wait_ready(e)
            s1 = st1.ap[0:1, :]
            e.activation(s1, s1, AF.Ln, bias=cst_sb.ap[0:1, 0:1])
            srr.acquire(e)
            bi = e.activation(srr.ap, s1, AF.Exp,
                              scale=cst_sb.ap[0:1, 2:3])
            tok = s_a.inc(bi)
            st1.consumed(s_a, tok)
            srr.produced(s_a, tok)
        A_(fa2)

        def ft2(e):
            ps_a.acquire(e)
            smu.wait_ready(e)
            srr.wait_ready(e)
            onesr_r.wait_ready(e)
            tok = None
            for j in range(NTC):
                ps_b[j].acquire(e)
                e.matmul(ps_a.ap[:, j * TCW:(j + 1) * TCW], onesr_r.ap,
                         smu.ap[:, j * TCW:(j + 1) * TCW], start=True, stop=True)
                bi2 = e.matmul(ps_b[j].ap, onesr_r.ap,
                               srr.ap[:, j * TCW:(j + 1) * TCW], start=True, stop=True)
                tok = s_t.inc(bi2)
            smu.consumed(s_t, tok)
            srr.consumed(s_t, tok)
            ps_a.produced(s_t, tok)
            for j in range(NTC):
                ps_b[j].produced(s_t, tok)
        T_(ft2)

    def p0_ln(b):
        for k in range(KD):
            load(pool[k], xfm[b, k])
        ln_stats_and_bcast(pool[0:KD], pool[11], pool[10])
        for k in range(KD):
            def fv3(e, k=k):
                ps_a.wait_ready(e)
                pool[10].acquire(e)
                e.tensor_sub(pool[10].ap, pool[k].ap, ps_a.ap)
                tok = None
                for j in range(NTC):
                    ps_b[j].wait_ready(e)
                    bi = e.tensor_mul(pool[k].ap[:, j * TCW:(j + 1) * TCW],
                                      pool[10].ap[:, j * TCW:(j + 1) * TCW],
                                      ps_b[j].ap)
                    tok = s_v.inc(bi)
                    ps_b[j].consumed(s_v, tok)
                ps_a.consumed(s_v, tok)
                pool[10].consumed(s_v, tok)
                pool[k].produced(s_v, tok)
            V_(fv3)
            store(pool[k], xn_d[b, k])

    def ffn_stage(b, sname, w_ap, in_d, inname, out_d, outname, is_last):
        slot_i = [0]
        for j in range(NTC):
            for k in range(KD):
                wslot = wt_f[k % 2]
                loadg(wslot, w_ap[k * 128:(k + 1) * 128, :])
                ldslot = ld_r[k % 2]
                rawkey = (inname, b, k, j)

                def fld(e, ldslot=ldslot, k=k, j=j, rawkey=rawkey):
                    ldslot.acquire(e)
                    if rawkey in store_toks:
                        ss, tt = store_toks[rawkey]
                        Buf._wait(e, ss, tt)
                    bi = e.dma_start(out=ldslot.ap[:, 0:TCW],
                                     in_=in_d[b, k, :, j * TCW:(j + 1) * TCW])
                    s = g_st.nxt()
                    ldslot.produced(s, s.inc(bi, 16))
                G_(fld)

                def ftk(e, k=k, wslot=wslot, ldslot=ldslot):
                    wslot.wait_ready(e)
                    ldslot.wait_ready(e)
                    tok = None
                    for m in range(KD):
                        if k == 0:
                            (ps_a if m < 4 else ps_b[m - 4]).acquire(e)
                        ob = (ps_a.ap[:, (m % 4) * TCW:(m % 4 + 1) * TCW]
                              if m < 4 else ps_b[m - 4].ap)
                        bi = e.matmul(ob, wslot.ap[:, m * 128:(m + 1) * 128],
                                      ldslot.ap[:, 0:TCW],
                                      start=(k == 0), stop=(k == KD - 1))
                        tok = s_t.inc(bi)
                    ldslot.consumed(s_t, tok)
                    wslot.consumed(s_t, tok)
                    if k == KD - 1:
                        ps_a.produced(s_t, tok)
                        for m in range(4):
                            ps_b[m].produced(s_t, tok)
                T_(ftk)

            for m in range(KD):
                slot = pool[8 + slot_i[0] % 3]
                slot_i[0] += 1

                def fe(e, m=m, slot=slot):
                    slot.acquire(e)
                    src = ps_a if m < 4 else ps_b[m - 4]
                    src.wait_ready(e)
                    sap = (ps_a.ap[:, (m % 4) * TCW:(m % 4 + 1) * TCW]
                           if m < 4 else ps_b[m - 4].ap)
                    if is_last:
                        ffnb_sb.wait_ready(e)
                        bi = e.activation(slot.ap[:, 0:TCW], sap, AF.Tanh,
                                          bias=ffnb_sb.ap[:, m:m + 1])
                        tok = s_a.inc(bi)
                        ffnb_sb.consumed(s_a, tok)
                    else:
                        bi = e.activation(slot.ap[:, 0:TCW], sap, AF.Copy)
                        tok = s_a.inc(bi)
                    src.consumed(s_a, tok)
                    slot.produced(s_a, tok)
                A_(fe)
                store(slot, out_d[b, m, :, j * TCW:(j + 1) * TCW],
                      src_ap=slot.ap[:, 0:TCW], key=(outname, b, m, j))

    def ffn(b):
        loads_barrier()
        ffn_stage(b, "s1", ffnw, xn_d, "xn", h1_d, "h1", False)
        flush()
        ffn_stage(b, "s2", ffna1, h1_d, "h1", h2_d, "h2", False)
        flush()
        ffn_stage(b, "s3", ffna2, h2_d, "h2", ffn_d, "ffn", True)

    def load_layer_weights(l):
        loadg(xpwT_sb, W[f"xpwT{l}"])
        load(convw_sb, W[f"convw{l}"])
        load(convb_sb, W[f"convb{l}"])
        load(dtb_sb, W[f"dtb{l}"])
        load(dsk_sb, W[f"dsk{l}"])
        load(lnw_sb, W[f"lnw{l}"])
        load(lnb_sb, W[f"lnb{l}"])
        load(A_sb, W[f"alog{l}"])

        def fa(e):
            A_sb.wait_ready(e)
            bi = e.activation(A_sb.ap, A_sb.ap, AF.Exp)
            A_sb.produced(s_a, s_a.inc(bi))
        A_(fa)

        def fv(e):
            A_sb.wait_ready(e)
            bi = e.tensor_scalar(A_sb.ap, A_sb.ap, -1.0, None, OP.mult)
            A_sb.produced(s_v, s_v.inc(bi))
        V_(fv)

    def mamba_A(l, b, stream_d):
        loads_barrier()
        for k in range(KD):
            load(pool[k], stream_d[b, k])
        for m in range(2 * KI):
            dtile = m % KI
            is_z = m >= KI

            def fw(e, m=m):
                wt_in.acquire(e)
                bi = e.dma_start(out=wt_in.ap, in_=W[f"inwT{l}"][m])
                s = g_ld.nxt()
                wt_in.produced(s, s.inc(bi, 16))
            L_(fw)

            def ft(e, m=m):
                wt_in.wait_ready(e)
                for k in range(KD):
                    pool[k].wait_ready(e)
                tok = None
                for j in range(NTC):
                    ps_b[j].acquire(e)
                    for k in range(KD):
                        bi = e.matmul(ps_b[j].ap,
                                      wt_in.ap[:, k * 128:(k + 1) * 128],
                                      pool[k].ap[:, j * TCW:(j + 1) * TCW],
                                      start=(k == 0), stop=(k == KD - 1))
                        tok = s_t.inc(bi)
                    ps_b[j].produced(s_t, tok)
                wt_in.consumed(s_t, tok)
                for k in range(KD):
                    pool[k].consumed(s_t, tok)
            T_(ft)

            if not is_z:
                def fe(e):
                    conv_sb.acquire(e)
                    tok = None
                    for j in range(NTC):
                        ps_b[j].wait_ready(e)
                        bi = e.activation(
                            conv_sb.ap[:, j * TCW:(j + 1) * TCW],
                            ps_b[j].ap, AF.Copy)
                        tok = s_a.inc(bi)
                        ps_b[j].consumed(s_a, tok)
                    conv_sb.produced(s_a, tok)
                A_(fe)

                def fcv(e, dtile=dtile):
                    conv_sb.wait_ready(e)
                    convw_sb.wait_ready(e)
                    pool[9].acquire(e)
                    e.tensor_scalar(pool[9].ap, conv_sb.ap[:, 0:T],
                                    convw_sb.ap[:, dtile * 4 + 3:dtile * 4 + 4],
                                    None, OP.mult)
                    bi = None
                    for tap in (0, 1, 2):
                        sh = 3 - tap
                        bi = e.scalar_tensor_tensor(
                            pool[9].ap[:, sh:T], conv_sb.ap[:, 0:T - sh],
                            convw_sb.ap[:, dtile * 4 + tap:dtile * 4 + tap + 1],
                            pool[9].ap[:, sh:T], OP.mult, OP.add)
                    tok = s_v.inc(bi)
                    conv_sb.consumed(s_v, tok)
                    convw_sb.consumed(s_v, tok)
                    pool[9].produced(s_v, tok)
                V_(fcv)

                def fu(e, dtile=dtile):
                    pool[9].wait_ready(e)
                    convb_sb.wait_ready(e)
                    pool[10].acquire(e)
                    bi = e.activation(pool[10].ap, pool[9].ap, AF.Silu,
                                      bias=convb_sb.ap[:, dtile:dtile + 1])
                    tok = s_a.inc(bi)
                    pool[9].consumed(s_a, tok)
                    convb_sb.consumed(s_a, tok)
                    pool[10].produced(s_a, tok)
                A_(fu)
                store(pool[10], u_d[b, dtile])

                def fur(e, dtile=dtile):
                    pool[9].wait_ready(e)
                    u_r.acquire(e)
                    bi = e.activation(u_r.ap, pool[9].ap, AF.Silu,
                                      bias=convb_sb.ap[:, dtile:dtile + 1])
                    tok = s_a.inc(bi)
                    pool[9].consumed(s_a, tok)
                    u_r.produced(s_a, tok)
                A_(fur)

                def fxp(e, dtile=dtile):
                    u_r.wait_ready(e)
                    xpwT_sb.wait_ready(e)
                    if dtile == 0:
                        ps_a.acquire(e)
                    tok = None
                    for j in range(NTC):
                        bi = e.matmul(ps_a.ap[0:96, j * TCW:(j + 1) * TCW],
                                      xpwT_sb.ap[:, dtile * 96:(dtile + 1) * 96],
                                      u_r.ap[:, j * TCW:(j + 1) * TCW],
                                      start=(dtile == 0), stop=(dtile == KI - 1))
                        tok = s_t.inc(bi)
                    u_r.consumed(s_t, tok)
                    xpwT_sb.consumed(s_t, tok)
                    if dtile == KI - 1:
                        ps_a.produced(s_t, tok)
                T_(fxp)
            else:
                def fz(e, dtile=dtile):
                    pool[10].acquire(e)
                    tok = None
                    for j in range(NTC):
                        ps_b[j].wait_ready(e)
                        bi = e.activation(pool[10].ap[:, j * TCW:(j + 1) * TCW],
                                          ps_b[j].ap, AF.Silu)
                        tok = s_a.inc(bi)
                        ps_b[j].consumed(s_a, tok)
                    pool[10].produced(s_a, tok)
                A_(fz)
                store(pool[10], zs_d[b, dtile])

        def fxe(e, b=b):
            ps_a.wait_ready(e)
            xdbl[b].acquire(e)
            bi = e.activation(xdbl[b].ap, ps_a.ap[0:96, :], AF.Copy)
            tok = s_a.inc(bi)
            ps_a.consumed(s_a, tok)
            xdbl[b].produced(s_a, tok)
        A_(fxe)

    def mamba_scan(l, b):
        loads_barrier()
        for g in range(NGRP):
            for i in range(G):
                dtile = g * G + i

                loadg(dtw1, W[f"dtwT{l}"][:, dtile * 128:(dtile + 1) * 128])

                def fdt(e, dtile=dtile):
                    dtw1.wait_ready(e)
                    xdbl[b].wait_ready(e)
                    tok = None
                    for j in range(NTC):
                        ps_b[j].acquire(e)
                        bi = e.matmul(ps_b[j].ap,
                                      dtw1.ap,
                                      xdbl[b].ap[0:64, j * TCW:(j + 1) * TCW],
                                      start=True, stop=True)
                        tok = s_t.inc(bi)
                        ps_b[j].produced(s_t, tok)
                    xdbl[b].consumed(s_t, tok)
                    dtw1.consumed(s_t, tok)
                T_(fdt)

                def fsp(e, i=i, dtile=dtile):
                    pool[9].acquire(e)
                    dtb_sb.wait_ready(e)
                    tok = None
                    for j in range(NTC):
                        ps_b[j].wait_ready(e)
                        bi = e.activation(pool[9].ap[:, j * TCW:(j + 1) * TCW],
                                          ps_b[j].ap, AF.Exp,
                                          bias=dtb_sb.ap[:, dtile:dtile + 1])
                        tok = s_a.inc(bi)
                        ps_b[j].consumed(s_a, tok)
                    dtb_sb.consumed(s_a, tok)
                    pool[4 + i].acquire(e)
                    bi = e.activation(pool[4 + i].ap, pool[9].ap, AF.Ln, bias=cst_sb.ap[:, 1:2])
                    tok = s_a.inc(bi)
                    pool[9].consumed(s_a, tok)
                    pool[4 + i].produced(s_a, tok)
                A_(fsp)

                load(pool[0 + i], u_d[b, dtile])

                def fdtu(e, i=i):
                    pool[4 + i].wait_ready(e)
                    pool[0 + i].wait_ready(e)
                    pool[6 + i].acquire(e)
                    bi = e.tensor_mul(pool[6 + i].ap, pool[4 + i].ap, pool[0 + i].ap)
                    tok = s_v.inc(bi)
                    pool[4 + i].consumed(s_v, tok)
                    pool[0 + i].consumed(s_v, tok)
                    pool[6 + i].produced(s_v, tok)
                V_(fdtu)

            def em_stage(n):
                def fstg(e, n=n, b=b):
                    stgB.acquire(e)
                    stgC.acquire(e)
                    xdbl[b].wait_ready(e)
                    s1 = g_sc.nxt()
                    bi = e.dma_start(out=stgB.ap, in_=xdbl[b].ap[64 + n:65 + n, :])
                    t1 = s1.inc(bi, 16)
                    s2 = g_sc.nxt()
                    bi = e.dma_start(out=stgC.ap, in_=xdbl[b].ap[80 + n:81 + n, :])
                    t2 = s2.inc(bi, 16)
                    stgB.produced(s1, t1)
                    stgC.produced(s2, t2)
                    xdbl[b].consumed(s2, t2)
                A_(fstg)

            def em_bmm(stg, dstb):
                def fbc(e, stg=stg):
                    stg.wait_ready(e)
                    onesr_r.wait_ready(e)
                    tok = None
                    for j in range(NTC):
                        ps_b[j].acquire(e)
                        bi = e.matmul(ps_b[j].ap, onesr_r.ap,
                                      stg.ap[:, j * TCW:(j + 1) * TCW],
                                      start=True, stop=True)
                        tok = s_t.inc(bi)
                        ps_b[j].produced(s_t, tok)
                    stg.consumed(s_t, tok)
                T_(fbc)

            def em_bev(dstb):
                def fev(e, dstb=dstb):
                    dstb.acquire(e)
                    tok = None
                    for j in range(NTC):
                        ps_b[j].wait_ready(e)
                        bi = e.activation(dstb.ap[:, j * TCW:(j + 1) * TCW],
                                          ps_b[j].ap, AF.Copy)
                        tok = s_a.inc(bi)
                        ps_b[j].consumed(s_a, tok)
                    dstb.produced(s_a, tok)
                A_(fev)

            def em_da(n, i):
                dtile = g * G + i
                ci = dtile * NST + n
                def fda(e, i=i, ci=ci):
                    pool[8].acquire(e)
                    pool[4 + i].wait_ready(e)
                    A_sb.wait_ready(e)
                    bi = e.activation(pool[8].ap, pool[4 + i].ap, AF.Exp,
                                      scale=A_sb.ap[:, ci:ci + 1])
                    tok = s_a.inc(bi)
                    pool[4 + i].consumed(s_a, tok)
                    A_sb.consumed(s_a, tok)
                    pool[8].produced(s_a, tok)
                A_(fda)

            def em_dve(n, i):
                pslot = pool[9] if (n * G + i) % 2 == 0 else pool[12]
                def fscan(e, i=i, n=n, pslot=pslot):
                    pslot.acquire(e)
                    ld_r[0].wait_ready(e)
                    pool[6 + i].wait_ready(e)
                    Bb32 = ld_r[0].ap.bitcast(F32)
                    bi = e.tensor_mul(pslot.ap, pool[6 + i].ap, Bb32)
                    tok = s_v.inc(bi)
                    ld_r[0].consumed(s_v, tok)
                    pool[6 + i].consumed(s_v, tok)
                    pslot.produced(s_v, tok)
                    pool[8].wait_ready(e)
                    pool[10].acquire(e)
                    bi = e.tensor_tensor_scan(pool[10].ap, pool[8].ap,
                                              pslot.ap, 0.0, OP.mult, OP.add)
                    tok = s_v.inc(bi)
                    pool[8].consumed(s_v, tok)
                    pslot.consumed(s_v, tok)
                    pool[10].produced(s_v, tok)
                    ld_r[1].wait_ready(e)
                    Cb32 = ld_r[1].ap.bitcast(F32)
                    if n == 0:
                        pool[2 + i].acquire(e)
                        bi = e.tensor_mul(pool[2 + i].ap, pool[10].ap, Cb32)
                        tok = s_v.inc(bi)
                        pool[2 + i].produced(s_v, tok)
                    else:
                        bi = e.tensor_mul(pslot.ap, pool[10].ap, Cb32)
                        tok = s_v.inc(bi)
                        pslot.produced(s_v, tok)
                    ld_r[1].consumed(s_v, tok)
                    pool[10].consumed(s_v, tok)
                V_(fscan)
                if n > 0:
                    def fadd(e, i=i, pslot=pslot):
                        pslot.wait_ready(e)
                        pool[2 + i].wait_ready(e)
                        bi = e.tensor_add(pool[2 + i].ap, pool[2 + i].ap,
                                          pslot.ap)
                        tok = s_g.inc(bi)
                        pslot.consumed(s_g, tok)
                        pool[2 + i].produced(s_g, tok)
                    G_(fadd)

            em_stage(0)
            em_bmm(stgB, ld_r[0])
            em_bev(ld_r[0])
            em_bmm(stgC, ld_r[1])
            em_bev(ld_r[1])
            em_stage(1)
            em_da(0, 0)
            for n in range(NST):
                em_dve(n, 0)
                em_da(n, 1)
                em_dve(n, 1)
                if n + 1 < NST:
                    em_bmm(stgB, ld_r[0])
                    em_bev(ld_r[0])
                    em_da(n + 1, 0)
                    em_bmm(stgC, ld_r[1])
                    em_bev(ld_r[1])
                if n + 2 < NST:
                    em_stage(n + 2)

            flush()
            for i in range(G):
                dtile = g * G + i
                load(pool[11], zs_d[b, dtile])

                def fg(e, i=i, dtile=dtile):
                    pool[8].acquire(e)
                    pool[0 + i].wait_ready(e)
                    pool[2 + i].wait_ready(e)
                    dsk_sb.wait_ready(e)
                    bi = e.scalar_tensor_tensor(
                        pool[8].ap, pool[0 + i].ap,
                        dsk_sb.ap[:, dtile:dtile + 1], pool[2 + i].ap,
                        OP.mult, OP.add)
                    tok = s_v.inc(bi)
                    pool[0 + i].consumed(s_v, tok)
                    pool[2 + i].consumed(s_v, tok)
                    dsk_sb.consumed(s_v, tok)
                    pool[8].produced(s_v, tok)
                    pool[11].wait_ready(e)
                    pool[10].acquire(e)
                    bi = e.tensor_mul(pool[10].ap, pool[8].ap, pool[11].ap)
                    tok = s_v.inc(bi)
                    pool[8].consumed(s_v, tok)
                    pool[11].consumed(s_v, tok)
                    pool[10].produced(s_v, tok)
                V_(fg)
                store(pool[10], yp_d[b, dtile])

    def mamba_out(l, b, stream_d, next_d):
        loads_barrier()
        for mp in range(4):
            def fw(e, mp=mp):
                for mi in range(2):
                    m = mp * 2 + mi
                    wt_out[mi].acquire(e)
                    bi = e.dma_start(out=wt_out[mi].ap, in_=W[f"outwT{l}"][m])
                    s = g_st.nxt()
                    wt_out[mi].produced(s, s.inc(bi, 16))
            G_(fw)

            for k in range(KI):
                ldslot = ld_r[k % 2]
                loadg(ldslot, yp_d[b, k])

                def ftk(e, k=k, ldslot=ldslot):
                    wt_out[0].wait_ready(e)
                    wt_out[1].wait_ready(e)
                    ldslot.wait_ready(e)
                    tok = None
                    for mi in range(2):
                        for j in range(NTC):
                            if k == 0:
                                (ps_a if mi == 0 else ps_b[j]).acquire(e)
                            ob = (ps_a.ap[:, j * TCW:(j + 1) * TCW] if mi == 0
                                  else ps_b[j].ap)
                            bi = e.matmul(ob,
                                          wt_out[mi].ap[:, k * 128:(k + 1) * 128],
                                          ldslot.ap[:, j * TCW:(j + 1) * TCW],
                                          start=(k == 0), stop=(k == KI - 1))
                            tok = s_t.inc(bi)
                    ldslot.consumed(s_t, tok)
                    wt_out[0].consumed(s_t, tok)
                    wt_out[1].consumed(s_t, tok)
                    if k == KI - 1:
                        ps_a.produced(s_t, tok)
                        for j in range(NTC):
                            ps_b[j].produced(s_t, tok)
                T_(ftk)

            for mi in range(2):
                m = mp * 2 + mi

                def fe(e, m=m, mi=mi):
                    pool[m].acquire(e)
                    tok = None
                    if mi == 0:
                        ps_a.wait_ready(e)
                        bi = e.activation(pool[m].ap, ps_a.ap, AF.Copy)
                        tok = s_a.inc(bi)
                        ps_a.consumed(s_a, tok)
                    else:
                        for j in range(NTC):
                            ps_b[j].wait_ready(e)
                            bi = e.activation(pool[m].ap[:, j * TCW:(j + 1) * TCW],
                                              ps_b[j].ap, AF.Copy)
                            tok = s_a.inc(bi)
                            ps_b[j].consumed(s_a, tok)
                    pool[m].produced(s_a, tok)
                A_(fe)

        ln_stats_and_bcast(pool[0:KD], pool[10], pool[11])
        for k in range(KD):
            load(pool[11], stream_d[b, k])

            def fv3(e, k=k):
                ps_a.wait_ready(e)
                pool[10].acquire(e)
                e.tensor_sub(pool[10].ap, pool[k].ap, ps_a.ap)
                tok = None
                for j in range(NTC):
                    ps_b[j].wait_ready(e)
                    bi = e.tensor_mul(pool[10].ap[:, j * TCW:(j + 1) * TCW],
                                      pool[10].ap[:, j * TCW:(j + 1) * TCW],
                                      ps_b[j].ap)
                    tok = s_v.inc(bi)
                    ps_b[j].consumed(s_v, tok)
                ps_a.consumed(s_v, tok)
                pool[k].consumed(s_v, tok)
                lnw_sb.wait_ready(e)
                lnb_sb.wait_ready(e)
                e.tensor_scalar(pool[10].ap, pool[10].ap,
                                lnw_sb.ap[:, k:k + 1], lnb_sb.ap[:, k:k + 1],
                                OP.mult, OP.add)
                pool[11].wait_ready(e)
                bi = e.tensor_add(pool[10].ap, pool[10].ap, pool[11].ap)
                tok = s_v.inc(bi)
                lnw_sb.consumed(s_v, tok)
                lnb_sb.consumed(s_v, tok)
                pool[11].consumed(s_v, tok)
                pool[10].produced(s_v, tok)
            V_(fv3)
            store(pool[10], next_d[b, k])

    def final(b):
        loads_barrier()
        for k in range(KD):
            base = (k % 2) * 4
            i0, i1, i2, i3 = base, base + 1, base + 2, base + 3
            load(pool[i0], xfm[b, k])
            load(pool[i1], m2_d[b, k])
            load(pool[i2], ffn_d[b, k])

            def fv(e, i0=i0, i1=i1, i2=i2, i3=i3):
                pool[i0].wait_ready(e)
                pool[i1].wait_ready(e)
                pool[i2].wait_ready(e)
                pool[i3].acquire(e)
                e.tensor_add(pool[i3].ap, pool[i0].ap, pool[i1].ap)
                bi = e.tensor_add(pool[i3].ap, pool[i3].ap, pool[i2].ap)
                tok = s_v.inc(bi)
                pool[i0].consumed(s_v, tok)
                pool[i1].consumed(s_v, tok)
                pool[i2].consumed(s_v, tok)
                pool[i3].produced(s_v, tok)
            V_(fv)
            store(pool[i3], outfm[b, k])

    for b in range(BC):
        p0_ln(b)
        flush()
        ffn(b)
        flush()
    for l in (0, 1):
        load_layer_weights(l)
        flush()
        for b in range(BC):
            mamba_A(l, b, xn_d if l == 0 else m1_d)
            flush()
            mamba_scan(l, b)
            mamba_out(l, b, xn_d if l == 0 else m1_d, m1_d if l == 0 else m2_d)
            flush()
    for b in range(BC):
        final(b)
    flush()

    def f_fin_g(e):
        for s in g_st.sems:
            e.wait_ge(s.h, s.n)
    G_(f_fin_g)
    def f_fin_l(e):
        for s in g_ld.sems:
            e.wait_ge(s.h, s.n)
    L_(f_fin_l)
    def f_fin_a(e):
        for s in g_sc.sems:
            e.wait_ge(s.h, s.n)
    A_(f_fin_a)

    flush()

    st.close()
    nc._kd_trace = trace_log
    return nc


def host_inputs(inputs, core):
    asc = np.ascontiguousarray
    x = np.asarray(inputs["L0_attn_x"])[2 * core:2 * core + 2]
    d = {
        "xfm": asc(x.transpose(0, 2, 1)).reshape(BC, KD, 128, T),
        "ffnw": asc(np.asarray(inputs["L0_ffn_weight"])),
        "ffna1": asc(np.asarray(inputs["L0_ffn_aux_weight1"])),
        "ffna2": asc(np.asarray(inputs["L0_ffn_aux_weight2"])),
        "ffnb": asc(np.asarray(inputs["L0_ffn_bias"]).reshape(8, 128).T),
        "onesr": np.ones((1, 128), np.float32),
        "onesc": np.ones((128, 1), np.float32),
        "cst": np.repeat(np.array([[EPS, 1.0, -0.5, 0.0]], np.float32), 128, axis=0),
    }
    for l in (0, 1):
        inwT = np.asarray(inputs["L0_attn_in_proj_w"])[l].T
        d[f"inwT{l}"] = asc(inwT.reshape(8, 128, 32, 128)
                            .transpose(2, 1, 0, 3)).reshape(32, 128, 8 * 128)
        outwT = np.asarray(inputs["L0_attn_out_proj_w"])[l].T
        d[f"outwT{l}"] = asc(outwT.reshape(16, 128, 8, 128)
                             .transpose(2, 1, 0, 3)).reshape(8, 128, 16 * 128)
        d[f"dtwT{l}"] = asc(np.asarray(inputs["L0_attn_dt_proj_w"])[l].T)
        d[f"xpwT{l}"] = asc(np.asarray(inputs["L0_attn_x_proj_w"])[l].T
                            .reshape(16, 128, 96).transpose(1, 0, 2)).reshape(128, 16 * 96)
        d[f"convw{l}"] = asc(np.asarray(inputs["L0_attn_conv_w"])[l]
                             .reshape(16, 128, 4).transpose(1, 0, 2)).reshape(128, 64)
        d[f"convb{l}"] = asc(np.asarray(inputs["L0_attn_conv_b"])[l].reshape(16, 128).T)
        d[f"dtb{l}"] = asc(np.asarray(inputs["L0_attn_dt_proj_b"])[l].reshape(16, 128).T)
        d[f"dsk{l}"] = asc(np.asarray(inputs["L0_attn_Dskip"])[l].reshape(16, 128).T)
        d[f"alog{l}"] = asc(np.asarray(inputs["L0_attn_A_log"])[l]
                            .reshape(16, 128, 16).transpose(1, 0, 2)).reshape(128, 256)
        d[f"lnw{l}"] = asc(np.asarray(inputs["L0_attn_ln_w"])[l].reshape(8, 128).T)
        d[f"lnb{l}"] = asc(np.asarray(inputs["L0_attn_ln_b"])[l].reshape(8, 128).T)
    return {k: np.ascontiguousarray(v, dtype=np.float32) for k, v in d.items()}


def kernel(**inputs):
    nc = build_nc()
    in_maps = [host_inputs(inputs, c) for c in range(NCORE)]
    res = run_bass_kernel_spmd(nc, in_maps, core_ids=list(range(NCORE)))
    outs = []
    for c in range(NCORE):
        o = np.asarray(res.results[c]["outfm"])
        o = o.reshape(BC, 1024, T).transpose(0, 2, 1)
        outs.append(o)
    return np.ascontiguousarray(np.concatenate(outs, axis=0), dtype=np.float32)


# version 07
# speedup vs baseline: 1.1317x; 1.0899x over previous
import numpy as np
from contextlib import ExitStack
import concourse.bass as bass
import concourse.mybir as mybir
from concourse.bass_utils import run_bass_kernel_spmd

F32 = mybir.dt.float32
F32R = mybir.dt.float32r
AF = mybir.ActivationFunctionType
OP = mybir.AluOpType

NCORE = 8
BC = 2
T = 2048
KD = 8
KI = 16
NST = 16
TCW = 512
NTC = T // TCW
G = 2
NGRP = KI // G
EPS = 1e-5


class Sem:
    def __init__(self, st, nc, name):
        self.h = st.enter_context(nc.semaphore(name))
        self.n = 0

    def inc(self, bi, amt=1):
        bi.then_inc(self.h, amt)
        self.n += amt
        return self.n


class SemGroup:
    def __init__(self, st, nc, name, k):
        self.sems = [Sem(st, nc, f"{name}{i}") for i in range(k)]
        self.i = 0

    def nxt(self):
        s = self.sems[self.i]
        self.i = (self.i + 1) % len(self.sems)
        return s


class Buf:
    CUR = ['?']
    WAITED = {}

    @classmethod
    def _wait(cls, eng, s, t):
        key = (cls.CUR[0], id(s))
        if cls.WAITED.get(key, -1) >= t:
            return
        eng.wait_ge(s.h, t)
        cls.WAITED[key] = t

    def __init__(self, ap):
        self.ap = ap
        self.ready = None
        self.cons = {}

    def acquire(self, eng):
        for s, t in self.cons.items():
            Buf._wait(eng, s, t)
        self.cons = {}

    def produced(self, s, tok):
        self.ready = (s, tok)

    def wait_ready(self, eng):
        if self.ready is not None:
            s, t = self.ready
            Buf._wait(eng, s, t)

    def consumed(self, s, tok):
        if s in self.cons:
            self.cons[s] = max(self.cons[s], tok)
        else:
            self.cons[s] = tok


def build_nc():
    nc = bass.Bass("TRN2", target_bir_lowering=False)

    def din(name, shape):
        return nc.dram_tensor(name, shape, F32, kind="ExternalInput").ap()

    def dout(name, shape):
        return nc.dram_tensor(name, shape, F32, kind="ExternalOutput").ap()

    def dint(name, shape):
        return nc.dram_tensor(name, shape, F32, kind="Internal").ap()

    xfm = din("xfm", [BC, KD, 128, T])
    W = {}
    for l in (0, 1):
        W[f"inwT{l}"] = din(f"inwT{l}", [32, 128, 8 * 128])
        W[f"outwT{l}"] = din(f"outwT{l}", [8, 128, 16 * 128])
        W[f"dtwT{l}"] = din(f"dtwT{l}", [64, 2048])
        W[f"xpwT{l}"] = din(f"xpwT{l}", [128, 16 * 96])
        W[f"convw{l}"] = din(f"convw{l}", [128, 64])
        W[f"convb{l}"] = din(f"convb{l}", [128, 16])
        W[f"dtb{l}"] = din(f"dtb{l}", [128, 16])
        W[f"dsk{l}"] = din(f"dsk{l}", [128, 16])
        W[f"alog{l}"] = din(f"alog{l}", [128, 256])
        W[f"lnw{l}"] = din(f"lnw{l}", [128, 8])
        W[f"lnb{l}"] = din(f"lnb{l}", [128, 8])
    ffnw = din("ffnw", [1024, 1024])
    ffna1 = din("ffna1", [1024, 1024])
    ffna2 = din("ffna2", [1024, 1024])
    ffnb = din("ffnb", [128, 8])
    onesr = din("onesr", [1, 128])
    onesc = din("onesc", [128, 1])
    cst = din("cst", [128, 4])

    outfm = dout("outfm", [BC, KD, 128, T])

    xn_d = dint("xn_d", [BC, KD, 128, T])
    ffn_d = dint("ffn_d", [BC, KD, 128, T])
    h1_d = dint("h1_d", [BC, KD, 128, T])
    h2_d = dint("h2_d", [BC, KD, 128, T])
    m1_d = dint("m1_d", [BC, KD, 128, T])
    m2_d = dint("m2_d", [BC, KD, 128, T])
    u_d = dint("u_d", [BC, KI, 128, T])
    zs_d = dint("zs_d", [BC, KI, 128, T])
    yp_d = dint("yp_d", [BC, KI, 128, T])

    st = ExitStack()
    st.__enter__()

    def mksb(name, shape, dt=F32):
        return Buf(st.enter_context(nc.sbuf_tensor(name, shape, dt)).ap())

    NPOOL = 13
    pool = [mksb(f"pl{i}", [128, T]) for i in range(NPOOL)]
    conv_sb = pool[11]
    xdbl = [mksb(f"xdbl{b}", [96, T], F32R) for b in range(BC)]
    wt_in = mksb("wt_in", [128, 1024])
    wt_f = [mksb("wt_f0", [128, 1024], F32R), mksb("wt_f1", [128, 1024], F32R)]
    ld_r = [mksb("ld_r0", [128, T], F32R), mksb("ld_r1", [128, T], F32R)]
    wt_out = [mksb("wt_out0", [128, 2048], F32R), mksb("wt_out1", [128, 2048], F32R)]
    xpwT_sb = mksb("xpwT_sb", [128, 1536], F32R)
    dtw1 = mksb("dtw1", [64, 128], F32R)
    u_r = mksb("u_r", [128, T], F32R)
    onesr_r = mksb("onesr_r", [1, 128], F32R)
    A_sb = mksb("A_sb", [128, 256])
    convw_sb = mksb("convw_sb", [128, 64])
    convb_sb = mksb("convb_sb", [128, 16])
    dtb_sb = mksb("dtb_sb", [128, 16])
    dsk_sb = mksb("dsk_sb", [128, 16])
    lnw_sb = mksb("lnw_sb", [128, 8])
    lnb_sb = mksb("lnb_sb", [128, 8])
    ffnb_sb = mksb("ffnb_sb", [128, 8])
    onesc_sb = mksb("onesc_sb", [128, 1])
    cst_sb = mksb("cst_sb", [128, 4])
    stgB = mksb("stgB", [1, T], F32R)
    stgC = mksb("stgC", [1, T], F32R)
    smu = mksb("smu", [1, T], F32R)
    srr = stgC

    ps_a = Buf(st.enter_context(nc.psum_tensor("ps_a", [128, T], F32)).ap())
    ps_b = [Buf(st.enter_context(nc.psum_tensor(f"ps_b{j}", [128, TCW], F32)).ap())
            for j in range(NTC)]

    g_ld = SemGroup(st, nc, "s_ld", 4)
    g_st = SemGroup(st, nc, "s_st", 4)
    g_sc = SemGroup(st, nc, "s_sc", 2)
    s_g = Sem(st, nc, "s_g")
    s_t = Sem(st, nc, "s_t")
    s_v = Sem(st, nc, "s_v")
    s_a = Sem(st, nc, "s_a")

    block = st.enter_context(nc.Block())

    Buf.WAITED = {}

    class RecInstr:
        __slots__ = ("incs",)
        def __init__(self):
            self.incs = []
        def then_inc(self, sem, amt):
            self.incs.append((sem, amt))
            return self

    class RecEng:
        __slots__ = ("lst", "kind")
        def __init__(self, lst, kind):
            self.lst = lst
            self.kind = kind
        def __getattr__(self, name):
            lst = self.lst
            kind = self.kind
            def call(*a, **kw):
                ri = RecInstr()
                lst.append((name, a, kw, ri))
                trace_log.append((kind, name, a, kw, ri))
                return ri
            return call

    rec = {k: [] for k in 'tvalg'}
    trace_log = []

    _eng_of = {}

    def flush():
        _eng_of.setdefault('t', block.tensor)
        for k, emit in (('t', block.tensor), ('v', block.vector),
                        ('a', block.scalar), ('l', block.sync),
                        ('g', block.gpsimd)):
            lst = rec[k]
            if not lst:
                continue
            rec[k] = []
            def f(e, lst=lst):
                for name, a, kw, ri in lst:
                    bi = getattr(e, name)(*a, **kw)
                    for sem, amt in ri.incs:
                        bi.then_inc(sem, amt)
            emit(f)

    def T_(fn):
        Buf.CUR[0] = 't'
        fn(RecEng(rec['t'], 't'))

    def V_(fn):
        Buf.CUR[0] = 'v'
        fn(RecEng(rec['v'], 'v'))

    def A_(fn):
        Buf.CUR[0] = 'a'
        fn(RecEng(rec['a'], 'a'))

    def L_(fn):
        Buf.CUR[0] = 'l'
        fn(RecEng(rec['l'], 'l'))

    def G_(fn):
        Buf.CUR[0] = 'g'
        fn(RecEng(rec['g'], 'g'))

    import os
    _PIN = os.environ.get("KD_PIN", "0") == "1"
    _GADD = os.environ.get("KD_GADD", "1") == "1"

    def pin_act(idx):
        if not _PIN:
            return
        def f(e):
            e.add_instruction(mybir.InstLoadActFuncSet(
                name=nc.get_next_instruction_name(),
                act_func_set_id=idx, ins=[], outs=[]))
        A_(f)

    def load(buf, src_ap, dst_ap=None):
        def f(e):
            buf.acquire(e)
            bi = e.dma_start(out=dst_ap if dst_ap is not None else buf.ap,
                             in_=src_ap)
            s = g_ld.nxt()
            buf.produced(s, s.inc(bi, 16))
        L_(f)

    def loadg(buf, src_ap, dst_ap=None):
        def f(e):
            buf.acquire(e)
            bi = e.dma_start(out=dst_ap if dst_ap is not None else buf.ap,
                             in_=src_ap)
            s = g_st.nxt()
            buf.produced(s, s.inc(bi, 16))
        G_(f)

    def store(buf, dst_ap, src_ap=None, key=None):
        def f(e):
            buf.wait_ready(e)
            bi = e.dma_start(out=dst_ap,
                             in_=src_ap if src_ap is not None else buf.ap)
            s = g_st.nxt()
            tok = s.inc(bi, 16)
            buf.consumed(s, tok)
            if key is not None:
                store_toks[key] = (s, tok)
        G_(f)

    store_toks = {}

    def loads_barrier():
        tgts = [(s, s.n) for s in g_st.sems]
        def fl(e):
            for s, t in tgts:
                Buf._wait(e, s, t)
        L_(fl)
        def fg(e):
            for s, t in tgts:
                Buf._wait(e, s, t)
        G_(fg)

    load(onesc_sb, onesc)
    load(ffnb_sb, ffnb)
    load(cst_sb, cst)
    loadg(onesr_r, onesr)

    def ln_stats_and_bcast(tiles, sq_slot, st_slot):
        st1 = st_slot
        for k in range(KD):
            def fa(e, k=k):
                sq_slot.acquire(e)
                tiles[k].wait_ready(e)
                bi = e.activation(sq_slot.ap, tiles[k].ap, AF.Square)
                tok = s_a.inc(bi)
                sq_slot.produced(s_a, tok)
                tiles[k].consumed(s_a, tok)
            A_(fa)

            def ft(e, k=k):
                tiles[k].wait_ready(e)
                sq_slot.wait_ready(e)
                if k == 0:
                    ps_a.acquire(e)
                    for j in range(NTC):
                        ps_b[j].acquire(e)
                tok = None
                for j in range(NTC):
                    e.matmul(ps_a.ap[0:1, j * TCW:(j + 1) * TCW],
                             onesc_sb.ap, tiles[k].ap[:, j * TCW:(j + 1) * TCW],
                             start=(k == 0), stop=(k == KD - 1))
                    bi2 = e.matmul(ps_b[j].ap[0:1, :],
                                   onesc_sb.ap, sq_slot.ap[:, j * TCW:(j + 1) * TCW],
                                   start=(k == 0), stop=(k == KD - 1))
                    tok = s_t.inc(bi2)
                sq_slot.consumed(s_t, tok)
                tiles[k].consumed(s_t, tok)
                if k == KD - 1:
                    ps_a.produced(s_t, tok)
                    for j in range(NTC):
                        ps_b[j].produced(s_t, tok)
            T_(ft)

        def fv(e):
            ps_a.wait_ready(e)
            for j in range(NTC):
                ps_b[j].wait_ready(e)
            smu.acquire(e)
            st1.acquire(e)
            pool[12].acquire(e)
            mu2 = pool[12].ap[0:1, :]
            s1 = st1.ap[0:1, :]
            e.tensor_scalar(smu.ap, ps_a.ap[0:1, :], 1.0 / 1024.0, None, OP.mult)
            e.tensor_scalar(mu2, ps_a.ap[0:1, :], 1.0 / 1024.0, None, OP.mult)
            for j in range(NTC):
                e.tensor_scalar(s1[:, j * TCW:(j + 1) * TCW],
                                ps_b[j].ap[0:1, :], 1.0 / 1024.0, None, OP.mult)
            e.tensor_mul(mu2, mu2, mu2)
            bi = e.tensor_sub(s1, s1, mu2)
            tok = s_v.inc(bi)
            ps_a.consumed(s_v, tok)
            for j in range(NTC):
                ps_b[j].consumed(s_v, tok)
            pool[12].consumed(s_v, tok)
            smu.produced(s_v, tok)
            st1.produced(s_v, tok)
        V_(fv)

        def fa2(e):
            st1.wait_ready(e)
            cst_sb.wait_ready(e)
            s1 = st1.ap[0:1, :]
            e.activation(s1, s1, AF.Ln, bias=cst_sb.ap[0:1, 0:1])
            srr.acquire(e)
            bi = e.activation(srr.ap, s1, AF.Exp,
                              scale=cst_sb.ap[0:1, 2:3])
            tok = s_a.inc(bi)
            st1.consumed(s_a, tok)
            srr.produced(s_a, tok)
        A_(fa2)

        def ft2(e):
            ps_a.acquire(e)
            smu.wait_ready(e)
            srr.wait_ready(e)
            onesr_r.wait_ready(e)
            tok = None
            for j in range(NTC):
                ps_b[j].acquire(e)
                e.matmul(ps_a.ap[:, j * TCW:(j + 1) * TCW], onesr_r.ap,
                         smu.ap[:, j * TCW:(j + 1) * TCW], start=True, stop=True)
                bi2 = e.matmul(ps_b[j].ap, onesr_r.ap,
                               srr.ap[:, j * TCW:(j + 1) * TCW], start=True, stop=True)
                tok = s_t.inc(bi2)
            smu.consumed(s_t, tok)
            srr.consumed(s_t, tok)
            ps_a.produced(s_t, tok)
            for j in range(NTC):
                ps_b[j].produced(s_t, tok)
        T_(ft2)

    def p0_ln(b):
        for k in range(KD):
            load(pool[k], xfm[b, k])
        ln_stats_and_bcast(pool[0:KD], pool[11], pool[10])
        for k in range(KD):
            def fv3(e, k=k):
                ps_a.wait_ready(e)
                pool[10].acquire(e)
                e.tensor_sub(pool[10].ap, pool[k].ap, ps_a.ap)
                tok = None
                for j in range(NTC):
                    ps_b[j].wait_ready(e)
                    bi = e.tensor_mul(pool[k].ap[:, j * TCW:(j + 1) * TCW],
                                      pool[10].ap[:, j * TCW:(j + 1) * TCW],
                                      ps_b[j].ap)
                    tok = s_v.inc(bi)
                    ps_b[j].consumed(s_v, tok)
                ps_a.consumed(s_v, tok)
                pool[10].consumed(s_v, tok)
                pool[k].produced(s_v, tok)
            V_(fv3)
            store(pool[k], xn_d[b, k])

    def ffn_stage(b, sname, w_ap, in_d, inname, out_d, outname, is_last):
        slot_i = [0]
        for j in range(NTC):
            for k in range(KD):
                wslot = wt_f[k % 2]
                loadg(wslot, w_ap[k * 128:(k + 1) * 128, :])
                ldslot = ld_r[k % 2]
                rawkey = (inname, b, k, j)

                def fld(e, ldslot=ldslot, k=k, j=j, rawkey=rawkey):
                    ldslot.acquire(e)
                    if rawkey in store_toks:
                        ss, tt = store_toks[rawkey]
                        Buf._wait(e, ss, tt)
                    bi = e.dma_start(out=ldslot.ap[:, 0:TCW],
                                     in_=in_d[b, k, :, j * TCW:(j + 1) * TCW])
                    s = g_st.nxt()
                    ldslot.produced(s, s.inc(bi, 16))
                G_(fld)

                def ftk(e, k=k, wslot=wslot, ldslot=ldslot):
                    wslot.wait_ready(e)
                    ldslot.wait_ready(e)
                    tok = None
                    for m in range(KD):
                        if k == 0:
                            (ps_a if m < 4 else ps_b[m - 4]).acquire(e)
                        ob = (ps_a.ap[:, (m % 4) * TCW:(m % 4 + 1) * TCW]
                              if m < 4 else ps_b[m - 4].ap)
                        bi = e.matmul(ob, wslot.ap[:, m * 128:(m + 1) * 128],
                                      ldslot.ap[:, 0:TCW],
                                      start=(k == 0), stop=(k == KD - 1))
                        tok = s_t.inc(bi)
                    ldslot.consumed(s_t, tok)
                    wslot.consumed(s_t, tok)
                    if k == KD - 1:
                        ps_a.produced(s_t, tok)
                        for m in range(4):
                            ps_b[m].produced(s_t, tok)
                T_(ftk)

            for m in range(KD):
                slot = pool[8 + slot_i[0] % 3]
                slot_i[0] += 1

                def fe(e, m=m, slot=slot):
                    slot.acquire(e)
                    src = ps_a if m < 4 else ps_b[m - 4]
                    src.wait_ready(e)
                    sap = (ps_a.ap[:, (m % 4) * TCW:(m % 4 + 1) * TCW]
                           if m < 4 else ps_b[m - 4].ap)
                    if is_last:
                        ffnb_sb.wait_ready(e)
                        bi = e.activation(slot.ap[:, 0:TCW], sap, AF.Tanh,
                                          bias=ffnb_sb.ap[:, m:m + 1])
                        tok = s_a.inc(bi)
                        ffnb_sb.consumed(s_a, tok)
                    else:
                        bi = e.activation(slot.ap[:, 0:TCW], sap, AF.Copy)
                        tok = s_a.inc(bi)
                    src.consumed(s_a, tok)
                    slot.produced(s_a, tok)
                A_(fe)
                store(slot, out_d[b, m, :, j * TCW:(j + 1) * TCW],
                      src_ap=slot.ap[:, 0:TCW], key=(outname, b, m, j))

    def ffn(b):
        loads_barrier()
        ffn_stage(b, "s1", ffnw, xn_d, "xn", h1_d, "h1", False)
        flush()
        ffn_stage(b, "s2", ffna1, h1_d, "h1", h2_d, "h2", False)
        flush()
        ffn_stage(b, "s3", ffna2, h2_d, "h2", ffn_d, "ffn", True)

    def load_layer_weights(l):
        loadg(xpwT_sb, W[f"xpwT{l}"])
        load(convw_sb, W[f"convw{l}"])
        load(convb_sb, W[f"convb{l}"])
        load(dtb_sb, W[f"dtb{l}"])
        load(dsk_sb, W[f"dsk{l}"])
        load(lnw_sb, W[f"lnw{l}"])
        load(lnb_sb, W[f"lnb{l}"])
        load(A_sb, W[f"alog{l}"])

        def fa(e):
            A_sb.wait_ready(e)
            bi = e.activation(A_sb.ap, A_sb.ap, AF.Exp)
            A_sb.produced(s_a, s_a.inc(bi))
        A_(fa)

        def fv(e):
            A_sb.wait_ready(e)
            bi = e.tensor_scalar(A_sb.ap, A_sb.ap, -1.0, None, OP.mult)
            A_sb.produced(s_v, s_v.inc(bi))
        V_(fv)

    def mamba_A(l, b, stream_d):
        loads_barrier()
        for k in range(KD):
            load(pool[k], stream_d[b, k])
        for m in range(2 * KI):
            dtile = m % KI
            is_z = m >= KI

            def fw(e, m=m):
                wt_in.acquire(e)
                bi = e.dma_start(out=wt_in.ap, in_=W[f"inwT{l}"][m])
                s = g_ld.nxt()
                wt_in.produced(s, s.inc(bi, 16))
            L_(fw)

            def ft(e, m=m):
                wt_in.wait_ready(e)
                for k in range(KD):
                    pool[k].wait_ready(e)
                tok = None
                for j in range(NTC):
                    ps_b[j].acquire(e)
                    for k in range(KD):
                        bi = e.matmul(ps_b[j].ap,
                                      wt_in.ap[:, k * 128:(k + 1) * 128],
                                      pool[k].ap[:, j * TCW:(j + 1) * TCW],
                                      start=(k == 0), stop=(k == KD - 1))
                        tok = s_t.inc(bi)
                    ps_b[j].produced(s_t, tok)
                wt_in.consumed(s_t, tok)
                for k in range(KD):
                    pool[k].consumed(s_t, tok)
            T_(ft)

            if not is_z:
                def fe(e):
                    conv_sb.acquire(e)
                    tok = None
                    for j in range(NTC):
                        ps_b[j].wait_ready(e)
                        bi = e.activation(
                            conv_sb.ap[:, j * TCW:(j + 1) * TCW],
                            ps_b[j].ap, AF.Copy)
                        tok = s_a.inc(bi)
                        ps_b[j].consumed(s_a, tok)
                    conv_sb.produced(s_a, tok)
                A_(fe)

                def fcv(e, dtile=dtile):
                    conv_sb.wait_ready(e)
                    convw_sb.wait_ready(e)
                    pool[9].acquire(e)
                    e.tensor_scalar(pool[9].ap, conv_sb.ap[:, 0:T],
                                    convw_sb.ap[:, dtile * 4 + 3:dtile * 4 + 4],
                                    None, OP.mult)
                    bi = None
                    for tap in (0, 1, 2):
                        sh = 3 - tap
                        bi = e.scalar_tensor_tensor(
                            pool[9].ap[:, sh:T], conv_sb.ap[:, 0:T - sh],
                            convw_sb.ap[:, dtile * 4 + tap:dtile * 4 + tap + 1],
                            pool[9].ap[:, sh:T], OP.mult, OP.add)
                    tok = s_v.inc(bi)
                    conv_sb.consumed(s_v, tok)
                    convw_sb.consumed(s_v, tok)
                    pool[9].produced(s_v, tok)
                V_(fcv)

                def fu(e, dtile=dtile):
                    pool[9].wait_ready(e)
                    convb_sb.wait_ready(e)
                    pool[10].acquire(e)
                    bi = e.activation(pool[10].ap, pool[9].ap, AF.Silu,
                                      bias=convb_sb.ap[:, dtile:dtile + 1])
                    tok = s_a.inc(bi)
                    pool[9].consumed(s_a, tok)
                    convb_sb.consumed(s_a, tok)
                    pool[10].produced(s_a, tok)
                A_(fu)
                store(pool[10], u_d[b, dtile])

                def fur(e, dtile=dtile):
                    pool[9].wait_ready(e)
                    u_r.acquire(e)
                    bi = e.activation(u_r.ap, pool[9].ap, AF.Silu,
                                      bias=convb_sb.ap[:, dtile:dtile + 1])
                    tok = s_a.inc(bi)
                    pool[9].consumed(s_a, tok)
                    u_r.produced(s_a, tok)
                A_(fur)

                def fxp(e, dtile=dtile):
                    u_r.wait_ready(e)
                    xpwT_sb.wait_ready(e)
                    if dtile == 0:
                        ps_a.acquire(e)
                    tok = None
                    for j in range(NTC):
                        bi = e.matmul(ps_a.ap[0:96, j * TCW:(j + 1) * TCW],
                                      xpwT_sb.ap[:, dtile * 96:(dtile + 1) * 96],
                                      u_r.ap[:, j * TCW:(j + 1) * TCW],
                                      start=(dtile == 0), stop=(dtile == KI - 1))
                        tok = s_t.inc(bi)
                    u_r.consumed(s_t, tok)
                    xpwT_sb.consumed(s_t, tok)
                    if dtile == KI - 1:
                        ps_a.produced(s_t, tok)
                T_(fxp)
            else:
                def fz(e, dtile=dtile):
                    pool[10].acquire(e)
                    tok = None
                    for j in range(NTC):
                        ps_b[j].wait_ready(e)
                        bi = e.activation(pool[10].ap[:, j * TCW:(j + 1) * TCW],
                                          ps_b[j].ap, AF.Silu)
                        tok = s_a.inc(bi)
                        ps_b[j].consumed(s_a, tok)
                    pool[10].produced(s_a, tok)
                A_(fz)
                store(pool[10], zs_d[b, dtile])

        def fxe(e, b=b):
            ps_a.wait_ready(e)
            xdbl[b].acquire(e)
            bi = e.activation(xdbl[b].ap, ps_a.ap[0:96, :], AF.Copy)
            tok = s_a.inc(bi)
            ps_a.consumed(s_a, tok)
            xdbl[b].produced(s_a, tok)
        A_(fxe)

    def mamba_scan(l, b):
        loads_barrier()
        for g in range(NGRP):
            for i in range(G):
                dtile = g * G + i

                loadg(dtw1, W[f"dtwT{l}"][:, dtile * 128:(dtile + 1) * 128])

                def fdt(e, dtile=dtile):
                    dtw1.wait_ready(e)
                    xdbl[b].wait_ready(e)
                    tok = None
                    for j in range(NTC):
                        ps_b[j].acquire(e)
                        bi = e.matmul(ps_b[j].ap,
                                      dtw1.ap,
                                      xdbl[b].ap[0:64, j * TCW:(j + 1) * TCW],
                                      start=True, stop=True)
                        tok = s_t.inc(bi)
                        ps_b[j].produced(s_t, tok)
                    xdbl[b].consumed(s_t, tok)
                    dtw1.consumed(s_t, tok)
                T_(fdt)

                def fsp(e, i=i, dtile=dtile):
                    pool[9].acquire(e)
                    dtb_sb.wait_ready(e)
                    tok = None
                    for j in range(NTC):
                        ps_b[j].wait_ready(e)
                        bi = e.activation(pool[9].ap[:, j * TCW:(j + 1) * TCW],
                                          ps_b[j].ap, AF.Exp,
                                          bias=dtb_sb.ap[:, dtile:dtile + 1])
                        tok = s_a.inc(bi)
                        ps_b[j].consumed(s_a, tok)
                    dtb_sb.consumed(s_a, tok)
                    pool[4 + i].acquire(e)
                    bi = e.activation(pool[4 + i].ap, pool[9].ap, AF.Ln, bias=cst_sb.ap[:, 1:2])
                    tok = s_a.inc(bi)
                    pool[9].consumed(s_a, tok)
                    pool[4 + i].produced(s_a, tok)
                A_(fsp)

                load(pool[0 + i], u_d[b, dtile])

                def fdtu(e, i=i):
                    pool[4 + i].wait_ready(e)
                    pool[0 + i].wait_ready(e)
                    pool[6 + i].acquire(e)
                    bi = e.tensor_mul(pool[6 + i].ap, pool[4 + i].ap, pool[0 + i].ap)
                    tok = s_v.inc(bi)
                    pool[4 + i].consumed(s_v, tok)
                    pool[0 + i].consumed(s_v, tok)
                    pool[6 + i].produced(s_v, tok)
                V_(fdtu)

            def em_stage(n):
                def fstg(e, n=n, b=b):
                    stgB.acquire(e)
                    stgC.acquire(e)
                    xdbl[b].wait_ready(e)
                    s1 = g_sc.nxt()
                    bi = e.dma_start(out=stgB.ap, in_=xdbl[b].ap[64 + n:65 + n, :])
                    t1 = s1.inc(bi, 16)
                    s2 = g_sc.nxt()
                    bi = e.dma_start(out=stgC.ap, in_=xdbl[b].ap[80 + n:81 + n, :])
                    t2 = s2.inc(bi, 16)
                    stgB.produced(s1, t1)
                    stgC.produced(s2, t2)
                    xdbl[b].consumed(s2, t2)
                A_(fstg)

            def em_bmm(stg, dstb):
                def fbc(e, stg=stg):
                    stg.wait_ready(e)
                    onesr_r.wait_ready(e)
                    tok = None
                    for j in range(NTC):
                        ps_b[j].acquire(e)
                        bi = e.matmul(ps_b[j].ap, onesr_r.ap,
                                      stg.ap[:, j * TCW:(j + 1) * TCW],
                                      start=True, stop=True)
                        tok = s_t.inc(bi)
                        ps_b[j].produced(s_t, tok)
                    stg.consumed(s_t, tok)
                T_(fbc)

            def em_bev(dstb):
                def fev(e, dstb=dstb):
                    dstb.acquire(e)
                    tok = None
                    for j in range(NTC):
                        ps_b[j].wait_ready(e)
                        bi = e.activation(dstb.ap[:, j * TCW:(j + 1) * TCW],
                                          ps_b[j].ap, AF.Copy)
                        tok = s_a.inc(bi)
                        ps_b[j].consumed(s_a, tok)
                    dstb.produced(s_a, tok)
                A_(fev)

            def em_da(n, i):
                dtile = g * G + i
                ci = dtile * NST + n
                def fda(e, i=i, ci=ci):
                    pool[8].acquire(e)
                    pool[4 + i].wait_ready(e)
                    A_sb.wait_ready(e)
                    bi = e.activation(pool[8].ap, pool[4 + i].ap, AF.Exp,
                                      scale=A_sb.ap[:, ci:ci + 1])
                    tok = s_a.inc(bi)
                    pool[4 + i].consumed(s_a, tok)
                    A_sb.consumed(s_a, tok)
                    pool[8].produced(s_a, tok)
                A_(fda)

            def em_dve(n, i):
                pslot = pool[9] if (n * G + i) % 2 == 0 else pool[12]
                def fscan(e, i=i, n=n, pslot=pslot):
                    pslot.acquire(e)
                    ld_r[0].wait_ready(e)
                    pool[6 + i].wait_ready(e)
                    Bb32 = ld_r[0].ap.bitcast(F32)
                    bi = e.tensor_mul(pslot.ap, pool[6 + i].ap, Bb32)
                    tok = s_v.inc(bi)
                    ld_r[0].consumed(s_v, tok)
                    pool[6 + i].consumed(s_v, tok)
                    pslot.produced(s_v, tok)
                    pool[8].wait_ready(e)
                    pool[10].acquire(e)
                    bi = e.tensor_tensor_scan(pool[10].ap, pool[8].ap,
                                              pslot.ap, 0.0, OP.mult, OP.add)
                    tok = s_v.inc(bi)
                    pool[8].consumed(s_v, tok)
                    pslot.consumed(s_v, tok)
                    pool[10].produced(s_v, tok)
                    ld_r[1].wait_ready(e)
                    Cb32 = ld_r[1].ap.bitcast(F32)
                    if n == 0:
                        pool[2 + i].acquire(e)
                        bi = e.tensor_mul(pool[2 + i].ap, pool[10].ap, Cb32)
                        tok = s_v.inc(bi)
                        pool[2 + i].produced(s_v, tok)
                    else:
                        bi = e.tensor_mul(pslot.ap, pool[10].ap, Cb32)
                        tok = s_v.inc(bi)
                        pslot.produced(s_v, tok)
                    ld_r[1].consumed(s_v, tok)
                    pool[10].consumed(s_v, tok)
                V_(fscan)
                if n > 0:
                    def fadd(e, i=i, pslot=pslot):
                        pslot.wait_ready(e)
                        pool[2 + i].wait_ready(e)
                        bi = e.tensor_add(pool[2 + i].ap, pool[2 + i].ap,
                                          pslot.ap)
                        tok = s_v.inc(bi)
                        pslot.consumed(s_v, tok)
                        pool[2 + i].produced(s_v, tok)
                    V_(fadd)

            em_stage(0)
            em_bmm(stgB, ld_r[0])
            em_bev(ld_r[0])
            em_bmm(stgC, ld_r[1])
            em_bev(ld_r[1])
            em_stage(1)
            em_da(0, 0)
            for n in range(NST):
                em_dve(n, 0)
                em_da(n, 1)
                em_dve(n, 1)
                if n + 1 < NST:
                    em_bmm(stgB, ld_r[0])
                    em_bev(ld_r[0])
                    em_da(n + 1, 0)
                    em_bmm(stgC, ld_r[1])
                    em_bev(ld_r[1])
                if n + 2 < NST:
                    em_stage(n + 2)

            flush()
            for i in range(G):
                dtile = g * G + i
                load(pool[11], zs_d[b, dtile])

                def fg(e, i=i, dtile=dtile):
                    pool[8].acquire(e)
                    pool[0 + i].wait_ready(e)
                    pool[2 + i].wait_ready(e)
                    dsk_sb.wait_ready(e)
                    bi = e.scalar_tensor_tensor(
                        pool[8].ap, pool[0 + i].ap,
                        dsk_sb.ap[:, dtile:dtile + 1], pool[2 + i].ap,
                        OP.mult, OP.add)
                    tok = s_v.inc(bi)
                    pool[0 + i].consumed(s_v, tok)
                    pool[2 + i].consumed(s_v, tok)
                    dsk_sb.consumed(s_v, tok)
                    pool[8].produced(s_v, tok)
                    pool[11].wait_ready(e)
                    pool[10].acquire(e)
                    bi = e.tensor_mul(pool[10].ap, pool[8].ap, pool[11].ap)
                    tok = s_v.inc(bi)
                    pool[8].consumed(s_v, tok)
                    pool[11].consumed(s_v, tok)
                    pool[10].produced(s_v, tok)
                V_(fg)
                store(pool[10], yp_d[b, dtile])

    def mamba_out(l, b, stream_d, next_d):
        loads_barrier()
        for mp in range(4):
            def fw(e, mp=mp):
                for mi in range(2):
                    m = mp * 2 + mi
                    wt_out[mi].acquire(e)
                    bi = e.dma_start(out=wt_out[mi].ap, in_=W[f"outwT{l}"][m])
                    s = g_st.nxt()
                    wt_out[mi].produced(s, s.inc(bi, 16))
            G_(fw)

            for k in range(KI):
                ldslot = ld_r[k % 2]
                loadg(ldslot, yp_d[b, k])

                def ftk(e, k=k, ldslot=ldslot):
                    wt_out[0].wait_ready(e)
                    wt_out[1].wait_ready(e)
                    ldslot.wait_ready(e)
                    tok = None
                    for mi in range(2):
                        for j in range(NTC):
                            if k == 0:
                                (ps_a if mi == 0 else ps_b[j]).acquire(e)
                            ob = (ps_a.ap[:, j * TCW:(j + 1) * TCW] if mi == 0
                                  else ps_b[j].ap)
                            bi = e.matmul(ob,
                                          wt_out[mi].ap[:, k * 128:(k + 1) * 128],
                                          ldslot.ap[:, j * TCW:(j + 1) * TCW],
                                          start=(k == 0), stop=(k == KI - 1))
                            tok = s_t.inc(bi)
                    ldslot.consumed(s_t, tok)
                    wt_out[0].consumed(s_t, tok)
                    wt_out[1].consumed(s_t, tok)
                    if k == KI - 1:
                        ps_a.produced(s_t, tok)
                        for j in range(NTC):
                            ps_b[j].produced(s_t, tok)
                T_(ftk)

            for mi in range(2):
                m = mp * 2 + mi

                def fe(e, m=m, mi=mi):
                    pool[m].acquire(e)
                    tok = None
                    if mi == 0:
                        ps_a.wait_ready(e)
                        bi = e.activation(pool[m].ap, ps_a.ap, AF.Copy)
                        tok = s_a.inc(bi)
                        ps_a.consumed(s_a, tok)
                    else:
                        for j in range(NTC):
                            ps_b[j].wait_ready(e)
                            bi = e.activation(pool[m].ap[:, j * TCW:(j + 1) * TCW],
                                              ps_b[j].ap, AF.Copy)
                            tok = s_a.inc(bi)
                            ps_b[j].consumed(s_a, tok)
                    pool[m].produced(s_a, tok)
                A_(fe)

        ln_stats_and_bcast(pool[0:KD], pool[10], pool[11])
        for k in range(KD):
            load(pool[11], stream_d[b, k])

            def fv3(e, k=k):
                ps_a.wait_ready(e)
                pool[10].acquire(e)
                e.tensor_sub(pool[10].ap, pool[k].ap, ps_a.ap)
                tok = None
                for j in range(NTC):
                    ps_b[j].wait_ready(e)
                    bi = e.tensor_mul(pool[10].ap[:, j * TCW:(j + 1) * TCW],
                                      pool[10].ap[:, j * TCW:(j + 1) * TCW],
                                      ps_b[j].ap)
                    tok = s_v.inc(bi)
                    ps_b[j].consumed(s_v, tok)
                ps_a.consumed(s_v, tok)
                pool[k].consumed(s_v, tok)
                lnw_sb.wait_ready(e)
                lnb_sb.wait_ready(e)
                e.tensor_scalar(pool[10].ap, pool[10].ap,
                                lnw_sb.ap[:, k:k + 1], lnb_sb.ap[:, k:k + 1],
                                OP.mult, OP.add)
                pool[11].wait_ready(e)
                bi = e.tensor_add(pool[10].ap, pool[10].ap, pool[11].ap)
                tok = s_v.inc(bi)
                lnw_sb.consumed(s_v, tok)
                lnb_sb.consumed(s_v, tok)
                pool[11].consumed(s_v, tok)
                pool[10].produced(s_v, tok)
            V_(fv3)
            store(pool[10], next_d[b, k])

    def final(b):
        loads_barrier()
        for k in range(KD):
            base = (k % 2) * 4
            i0, i1, i2, i3 = base, base + 1, base + 2, base + 3
            load(pool[i0], xfm[b, k])
            load(pool[i1], m2_d[b, k])
            load(pool[i2], ffn_d[b, k])

            def fv(e, i0=i0, i1=i1, i2=i2, i3=i3):
                pool[i0].wait_ready(e)
                pool[i1].wait_ready(e)
                pool[i2].wait_ready(e)
                pool[i3].acquire(e)
                e.tensor_add(pool[i3].ap, pool[i0].ap, pool[i1].ap)
                bi = e.tensor_add(pool[i3].ap, pool[i3].ap, pool[i2].ap)
                tok = s_v.inc(bi)
                pool[i0].consumed(s_v, tok)
                pool[i1].consumed(s_v, tok)
                pool[i2].consumed(s_v, tok)
                pool[i3].produced(s_v, tok)
            V_(fv)
            store(pool[i3], outfm[b, k])

    for b in range(BC):
        p0_ln(b)
        flush()
        ffn(b)
        flush()
    for l in (0, 1):
        load_layer_weights(l)
        flush()
        for b in range(BC):
            mamba_A(l, b, xn_d if l == 0 else m1_d)
            flush()
            mamba_scan(l, b)
            mamba_out(l, b, xn_d if l == 0 else m1_d, m1_d if l == 0 else m2_d)
            flush()
    for b in range(BC):
        final(b)
    flush()

    def f_fin_g(e):
        for s in g_st.sems:
            e.wait_ge(s.h, s.n)
    G_(f_fin_g)
    def f_fin_l(e):
        for s in g_ld.sems:
            e.wait_ge(s.h, s.n)
    L_(f_fin_l)
    def f_fin_a(e):
        for s in g_sc.sems:
            e.wait_ge(s.h, s.n)
    A_(f_fin_a)

    flush()

    st.close()
    nc._kd_trace = trace_log
    return nc


def host_inputs(inputs, core):
    asc = np.ascontiguousarray
    x = np.asarray(inputs["L0_attn_x"])[2 * core:2 * core + 2]
    d = {
        "xfm": asc(x.transpose(0, 2, 1)).reshape(BC, KD, 128, T),
        "ffnw": asc(np.asarray(inputs["L0_ffn_weight"])),
        "ffna1": asc(np.asarray(inputs["L0_ffn_aux_weight1"])),
        "ffna2": asc(np.asarray(inputs["L0_ffn_aux_weight2"])),
        "ffnb": asc(np.asarray(inputs["L0_ffn_bias"]).reshape(8, 128).T),
        "onesr": np.ones((1, 128), np.float32),
        "onesc": np.ones((128, 1), np.float32),
        "cst": np.repeat(np.array([[EPS, 1.0, -0.5, 0.0]], np.float32), 128, axis=0),
    }
    for l in (0, 1):
        inwT = np.asarray(inputs["L0_attn_in_proj_w"])[l].T
        d[f"inwT{l}"] = asc(inwT.reshape(8, 128, 32, 128)
                            .transpose(2, 1, 0, 3)).reshape(32, 128, 8 * 128)
        outwT = np.asarray(inputs["L0_attn_out_proj_w"])[l].T
        d[f"outwT{l}"] = asc(outwT.reshape(16, 128, 8, 128)
                             .transpose(2, 1, 0, 3)).reshape(8, 128, 16 * 128)
        d[f"dtwT{l}"] = asc(np.asarray(inputs["L0_attn_dt_proj_w"])[l].T)
        d[f"xpwT{l}"] = asc(np.asarray(inputs["L0_attn_x_proj_w"])[l].T
                            .reshape(16, 128, 96).transpose(1, 0, 2)).reshape(128, 16 * 96)
        d[f"convw{l}"] = asc(np.asarray(inputs["L0_attn_conv_w"])[l]
                             .reshape(16, 128, 4).transpose(1, 0, 2)).reshape(128, 64)
        d[f"convb{l}"] = asc(np.asarray(inputs["L0_attn_conv_b"])[l].reshape(16, 128).T)
        d[f"dtb{l}"] = asc(np.asarray(inputs["L0_attn_dt_proj_b"])[l].reshape(16, 128).T)
        d[f"dsk{l}"] = asc(np.asarray(inputs["L0_attn_Dskip"])[l].reshape(16, 128).T)
        d[f"alog{l}"] = asc(np.asarray(inputs["L0_attn_A_log"])[l]
                            .reshape(16, 128, 16).transpose(1, 0, 2)).reshape(128, 256)
        d[f"lnw{l}"] = asc(np.asarray(inputs["L0_attn_ln_w"])[l].reshape(8, 128).T)
        d[f"lnb{l}"] = asc(np.asarray(inputs["L0_attn_ln_b"])[l].reshape(8, 128).T)
    return {k: np.ascontiguousarray(v, dtype=np.float32) for k, v in d.items()}


def kernel(**inputs):
    nc = build_nc()
    in_maps = [host_inputs(inputs, c) for c in range(NCORE)]
    res = run_bass_kernel_spmd(nc, in_maps, core_ids=list(range(NCORE)))
    outs = []
    for c in range(NCORE):
        o = np.asarray(res.results[c]["outfm"])
        o = o.reshape(BC, 1024, T).transpose(0, 2, 1)
        outs.append(o)
    return np.ascontiguousarray(np.concatenate(outs, axis=0), dtype=np.float32)
